# version 21
# baseline (speedup 1.0000x reference)
import contextlib
import numpy as np
import concourse.bass as bass
import concourse.mybir as mybir
from concourse.bass_utils import run_bass_kernel_spmd

F32 = mybir.dt.float32
BF16 = mybir.dt.bfloat16
AF = mybir.ActivationFunctionType
ALU = mybir.AluOpType
AX = mybir.AxisListType
LN2 = 0.6931471805599453

D = 1024
H = 16
DH = 64
FT = 16
NBLK = 32
VW = H * 65

ENGS = ("sync", "gpsimd", "vector", "scalar", "tensor")


class Sem:
    def __init__(self, handle, name):
        self.h = handle
        self.name = name
        self.n = 0


class Plan:
    def __init__(self):
        self.ops = {e: [] for e in ENGS}
        self.ev = {}

    def emit(self, eng, fn):
        self.ops[eng].append(fn)

    def event(self, name, sem, k=1):
        sem.n += k
        self.ev[name] = (sem, sem.n)

    def alias(self, name, other):
        self.ev[name] = self.ev[other]

    def wait(self, api, name):
        sem, v = self.ev[name]
        api.wait_ge(sem.h, v)


class PEGuard:

    def __init__(self, eng, sep_op):
        self._e = eng
        self._sep = sep_op
        self._cfg = None

    def __getattr__(self, name):
        return getattr(self._e, name)

    def transpose(self, *a, **k):
        self._cfg = None
        return self._e.transpose(*a, **k)

    def matmul(self, out, lhsT, rhs, **k):
        if k.get("is_transpose"):
            self._cfg = None
            return self._e.matmul(out, lhsT, rhs, **k)
        cfg = (lhsT.partition_size(), lhsT.base_partition())
        if (self._cfg is not None and cfg[0] == self._cfg[0]
                and cfg[1] != self._cfg[1]):
            assert k.get("start") is True, "PE separator needs start=True site"
            self._e.matmul(out[0:1, 0:1], self._sep, self._sep,
                           start=True, stop=True)
        self._cfg = cfg
        return self._e.matmul(out, lhsT, rhs, **k)


def build_program(T=2048):
    NT = T // 128
    NST = NT // 4
    nc = bass.Bass("TRN2", target_bir_lowering=False, debug=False)

    x_h = nc.dram_tensor("x", [T, D], F32, kind="ExternalInput").ap()
    nw_h = nc.dram_tensor("norm_w", [8, 128], F32, kind="ExternalInput").ap()
    wq_h = nc.dram_tensor("Wq", [D, D], F32, kind="ExternalInput").ap()
    wk_h = nc.dram_tensor("Wk", [D, D], F32, kind="ExternalInput").ap()
    wv_h = nc.dram_tensor("Wv", [D, D], F32, kind="ExternalInput").ap()
    wqf_h = nc.dram_tensor("Wqf", [DH, FT], F32, kind="ExternalInput").ap()
    wkf_h = nc.dram_tensor("Wkf", [DH, FT], F32, kind="ExternalInput").ap()
    wo_h = nc.dram_tensor("Wout", [2 * D, D], F32, kind="ExternalInput").ap()
    fw_h = nc.dram_tensor("fw", [D, D], F32, kind="ExternalInput").ap()
    fb_h = nc.dram_tensor("fb", [1, D], F32, kind="ExternalInput").ap()
    dw_h = nc.dram_tensor("dw", [D, NBLK], F32, kind="ExternalInput").ap()
    da_h = nc.dram_tensor("da", [1, 1], F32, kind="ExternalInput").ap()
    out_h = nc.dram_tensor("out", [T, D], F32, kind="ExternalOutput").ap()

    wqs_h = nc.dram_tensor("wq_s", [D, D], BF16, kind="Internal").ap()
    wks_h = nc.dram_tensor("wk_s", [D, D], BF16, kind="Internal").ap()
    wvs_h = nc.dram_tensor("wv_s", [D, D], BF16, kind="Internal").ap()
    wos_h = nc.dram_tensor("wo_s", [2 * D, D], BF16, kind="Internal").ap()
    wqt_h = nc.dram_tensor("wqt_s", [D, D], BF16, kind="Internal").ap()
    sc1_h = nc.dram_tensor("sc1_s", [1, 1], F32, kind="Internal").ap()
    sc2_h = nc.dram_tensor("sc2_s", [1, 1], F32, kind="Internal").ap()

    es = contextlib.ExitStack()
    with es:
        def sb(name, shape, dt=BF16):
            return es.enter_context(nc.sbuf_tensor(name, shape, dt))

        def psum(name, shape, dt=F32):
            return es.enter_context(nc.psum_tensor(name, shape, dt))

        def mksem(name):
            return Sem(es.enter_context(nc.semaphore(name)), name)

        iof = sb("iof", [128, 128], F32)
        iop = sb("iop", [128, 1], F32)
        eye = sb("eye", [128, 128], BF16)
        eyef = sb("eyef", [128, 128], F32)
        trilq = sb("trilq", [128, 128], BF16)
        maskA = sb("maskA", [128, 128], BF16)
        maskB = sb("maskB", [128, 128], BF16)
        cbias = sb("cbias", [128, 1], F32)
        bias_b = sb("bias_b", [128, 8], F32)
        dat = sb("dat", [128, 1], F32)
        fbt = sb("fbt", [1, D], BF16)
        nwt = sb("nwt", [128, 8], F32)
        ones1r = sb("ones1r", [1, 128], BF16)
        g2t = sb("g2t", [128, 2], BF16)
        st_s1 = sb("st_s1", [128, 1], F32)
        st_s2 = sb("st_s2", [128, 1], F32)
        st_m = sb("st_m", [128, 1], F32)
        st_v = sb("st_v", [128, 1], F32)
        st_t = sb("st_t", [128, 2], F32)
        st_rs = sb("st_rs", [128, 1], F32)
        st_c2 = sb("st_c2", [128, 1], F32)
        afin = sb("afin", [128, NT], F32)
        bfin = sb("bfin", [128, NT], F32)
        mxs = sb("mxs", [2, 20], F32)
        xt = sb("xt", [128, 2 * D], F32)
        xnb = sb("xnb", [128, D], F32)
        hb = sb("hb", [128, D], BF16)
        xqb = sb("xqb", [128, 3 * D], BF16)

        q_am = sb("q_am", [128, 2 * NBLK], F32)
        q_e = sb("q_e", [128, 2 * NBLK], F32)
        q_d = sb("q_d", [128, NBLK], F32)
        q_r2 = sb("q_r2", [128, 2 * NBLK], F32)
        q_sh = sb("q_sh", [128, 2 * NBLK], F32)
        q_ab = sb("q_ab", [128, 2 * D], F32)
        q_sg = sb("q_sg", [128, 2 * D], BF16)
        q_c = sb("q_c", [128, 512], F32)
        q_u = [sb(f"q_u{i}", [128, 512], BF16) for i in range(4)]

        hT = sb("hT", [128, 8 * 512], BF16)
        xqT = sb("xqT", [128, 8 * 512], BF16)
        qT = sb("qT", [128, 8 * 512], BF16)
        kT = sb("kT", [128, 8 * 512], BF16)
        kThal = sb("kThal", [128, 8 * 128], BF16)
        v65 = sb("v65", [128, 4 * VW], BF16)
        v65hal = sb("v65hal", [128, VW], BF16)
        qftm = sb("qftm", [128, 4 * 512], BF16)
        kftm = sb("kftm", [128, 4 * 512], BF16)
        qfFM = sb("qfFM", [64, 1024], BF16)
        kfFM = sb("kfFM", [64, 1024], BF16)

        psiq = sb("psiq", [128, H * 128], BF16)
        psik = sb("psik", [128, H * 128], BF16)
        phiq = sb("phiq", [128, H * 128], BF16)
        attA = sb("attA", [128, H * 128], BF16)
        attB = sb("attB", [128, H * 128], BF16)
        atil = sb("atil", [128, H * 128], BF16)
        kv_ab = sb("kv_ab", [128, H * 65], BF16)
        kva_sb = sb("kva_sb", [128, H * 65], F32)
        kvb_sb = sb("kvb_sb", [64, 8 * 65], F32)
        kv_bb = sb("kv_bb", [64, 8 * 65], BF16)
        lin_tm = sb("lin_tm", [128, 4 * D], BF16)
        win_tm = sb("win_tm", [128, 4 * D], BF16)
        zs = sb("zs", [128, H], F32)
        rz = sb("rz", [128, H], F32)
        sqs = sb("sqs", [128, 512], BF16)
        sqs2 = sb("sqs2", [128, 512], BF16)
        fmst = sb("fmst", [128, 2 * 2048], BF16)
        wst = sb("wst", [128, D], F32)
        wbf = sb("wbf", [128, 2 * D], BF16)
        wbf2 = sb("wbf2", [128, 2 * D], BF16)
        wqfbd = sb("wqfbd", [128, 64], BF16)
        xres = sb("xres", [128, D], F32)
        oout = sb("oout", [128, D], F32)

        pbt = psum("pbt", [128, 1024], BF16)
        pf = [psum(f"pf{i}", [128, 512], F32) for i in range(4)]
        kva_p = psum("kva_p", [128, 455], F32)
        kva_p2 = psum("kva_p2", [128, 455], F32)
        kvmix = psum("kvmix", [128, 512], F32)

        s_ld = mksem("s_ld")
        s_vd = mksem("s_vd")
        s_x2 = [mksem("s_xa"), mksem("s_xb")]
        s_sg = [mksem("s_sga"), mksem("s_sgb")]
        s_sv = [mksem(f"s_sv{i}") for i in range(4)]
        s_wb = [mksem("s_wba"), mksem("s_wbb")]
        s_v2 = [mksem("s_v2a"), mksem("s_v2b")]
        s_so = [mksem("s_soa"), mksem("s_sob")]
        s_cb = mksem("s_cb")
        s_xr = mksem("s_xr")
        s_dv = mksem("s_dv")
        s_ac = mksem("s_ac")
        s_gp = mksem("s_gp")
        s_pe = mksem("s_pe")
        s_st = mksem("s_st")

        P = Plan()
        slot_user = {}
        B_EPSN, B_C2, B_XE, B_XS, B_WE, B_WS, B_ONE, B_SPARE = range(8)

        fillers = []

        def drip(n=1):
            k = 0
            while fillers and k < n:
                fn, evn = fillers.pop(0)
                P.emit("vector", fn)
                P.event(evn, s_dv)
                k += 1

        qtile_hist = []
        xq_par = {}

        def quant_emit(tag, src_f32, e_lo, k2, wait_evs, delta_tile,
                       reader_ev):
            par = len(qtile_hist) % 2
            parx = len(qtile_hist) % 3
            xq_par[tag] = parx
            war = [qtile_hist[-3][1]] if len(qtile_hist) >= 3 else []
            prev_tag = qtile_hist[-2][0] if len(qtile_hist) >= 2 else None
            qtile_hist.append((tag, reader_ev))
            qab = q_ab[:, par * D:(par + 1) * D]
            qsg = q_sg[:, par * D:(par + 1) * D]
            qam = q_am[:, par * NBLK:(par + 1) * NBLK]
            qe = q_e[:, par * NBLK:(par + 1) * NBLK]
            qr2 = q_r2[:, par * NBLK:(par + 1) * NBLK]
            qsh = q_sh[:, par * NBLK:(par + 1) * NBLK]
            out_bf = xqb[:, parx * D:(parx + 1) * D]

            def a_q(a, wait_evs=tuple(wait_evs), prev_tag=prev_tag):
                for evn in wait_evs:
                    P.wait(a, evn)
                if prev_tag is not None:
                    P.wait(a, prev_tag)
                a.activation(out=qab, in_=src_f32, func=AF.Abs)
                a.activation(out=qsg, in_=src_f32, func=AF.Sign)
                if delta_tile is not None:
                    a.activation(out=q_d[:], in_=delta_tile, func=AF.Tanh,
                                 scale=4.0)
                a.drain()
                a.sem_inc(s_ac.h, 1)
            P.emit("scalar", a_q)
            P.event(tag + "_abs", s_ac)

            def v_qe(v):
                P.wait(v, tag + "_abs")
                v.tensor_reduce(out=qam,
                                in_=qab.rearrange("p (g k) -> p g k", k=32),
                                axis=AX.X, op=ALU.max)
                v.drain()
                v.tensor_scalar(out=qam, in0=qam, scalar1=2.0 ** -50,
                                scalar2=None, op0=ALU.max)
                v.drain()
                v.tensor_scalar(out=qe, in0=qam, scalar1=2.0 ** (e_lo + 1),
                                scalar2=None, op0=ALU.is_ge)
                for j in range(e_lo + 2, e_lo + 15):
                    v.drain()
                    v.scalar_tensor_tensor(out=qe, in0=qam, scalar=2.0 ** j,
                                           in1=qe, op0=ALU.is_ge, op1=ALU.add)
                v.drain()
                if delta_tile is not None:
                    v.scalar_tensor_tensor(out=qe, in0=q_d[:], scalar=0.5,
                                           in1=qe, op0=ALU.mult, op1=ALU.add)
                else:
                    v.tensor_scalar(out=qe, in0=qe, scalar1=dat[:, 0:1],
                                    scalar2=None, op0=ALU.add)
                v.drain()
                v.sem_inc(s_dv.h, 1)
            P.emit("vector", v_qe)
            P.event(tag + "_e", s_dv)

            def a_q2(a):
                P.wait(a, tag + "_e")
                a.activation(out=qr2, in_=qe, func=AF.Exp, scale=-LN2)
                a.activation(out=qsh, in_=qe, func=AF.Exp, scale=LN2)
                a.drain()
                a.sem_inc(s_ac.h, 1)
            P.emit("scalar", a_q2)
            P.event(tag + "_sc", s_ac)

            for hi in range(2):
                def v_qh(v, hi=hi, war=tuple(war)):
                    P.wait(v, tag + "_sc")
                    if hi == 0:
                        for evn in war:
                            if evn in P.ev:
                                P.wait(v, evn)
                    cl = hi * 512
                    hab = qab[:, cl:cl + 512].rearrange("p (g k) -> p g k",
                                                        k=32)
                    hsg = qsg[:, cl:cl + 512]
                    hr2 = qr2[:, hi * 16:(hi + 1) * 16]
                    hsh = qsh[:, hi * 16:(hi + 1) * 16]
                    v.scalar_tensor_tensor(
                        out=q_c[:].rearrange("p (g k) -> p g k", k=32),
                        in0=hab, scalar=float(k2),
                        in1=hr2[:, :, None].broadcast_to([128, 16, 32]),
                        op0=ALU.mult, op1=ALU.mult)
                    v.drain()
                    v.tensor_scalar(out=q_u[0][:], in0=q_c[:], scalar1=0.5,
                                    scalar2=None, op0=ALU.is_ge)
                    v.tensor_scalar(out=q_u[1][:], in0=q_c[:], scalar1=1.5,
                                    scalar2=None, op0=ALU.is_ge)
                    v.tensor_scalar(out=q_u[2][:], in0=q_c[:], scalar1=2.5,
                                    scalar2=None, op0=ALU.is_ge)
                    v.tensor_scalar(out=q_u[3][:], in0=q_c[:], scalar1=3.5,
                                    scalar2=None, op0=ALU.is_ge)
                    v.drain()
                    v.tensor_add(out=q_u[0][:], in0=q_u[0][:], in1=q_u[1][:])
                    v.tensor_add(out=q_u[2][:], in0=q_u[2][:], in1=q_u[3][:])
                    v.drain()
                    v.tensor_scalar(out=q_u[1][:], in0=q_c[:], scalar1=5.0,
                                    scalar2=2.0, op0=ALU.is_ge, op1=ALU.mult)
                    v.tensor_scalar(out=q_u[3][:], in0=q_c[:], scalar1=7.0,
                                    scalar2=2.0, op0=ALU.is_ge, op1=ALU.mult)
                    v.drain()
                    v.tensor_add(out=q_u[0][:], in0=q_u[0][:], in1=q_u[2][:])
                    v.tensor_add(out=q_u[1][:], in0=q_u[1][:], in1=q_u[3][:])
                    v.drain()
                    v.tensor_scalar(out=q_u[2][:], in0=q_c[:], scalar1=10.0,
                                    scalar2=4.0, op0=ALU.is_ge, op1=ALU.mult)
                    v.drain()
                    v.tensor_add(out=q_u[0][:], in0=q_u[0][:], in1=q_u[1][:])
                    v.drain()
                    v.tensor_add(out=q_u[0][:], in0=q_u[0][:], in1=q_u[2][:])
                    v.drain()
                    v.tensor_mul(out=q_u[0][:], in0=q_u[0][:], in1=hsg)
                    v.drain()
                    v.scalar_tensor_tensor(
                        out=out_bf[:, cl:cl + 512]
                        .rearrange("p (g k) -> p g k", k=32),
                        in0=q_u[0][:].rearrange("p (g k) -> p g k", k=32),
                        scalar=float(1.0 / k2),
                        in1=hsh[:, :, None].broadcast_to([128, 16, 32]),
                        op0=ALU.mult, op1=ALU.mult)
                    v.drain()
                    v.sem_inc(s_dv.h, 1)
                fillers.append((v_qh, tag + "_h0" if hi == 0 else tag))

        def g_iota(g):
            g.iota(iof[:], [[1, 128]], channel_multiplier=0,
                   allow_small_or_imprecise_dtypes=True)
            g.iota(iop[:], [[0, 1]], channel_multiplier=1,
                   allow_small_or_imprecise_dtypes=True).then_inc(s_gp.h, 1)
        P.emit("gpsimd", g_iota)
        P.event("iota", s_gp)

        def s_small(sy):
            sy.dma_start(out=nwt[:], in_=nw_h.transpose([1, 0]), allow_slow_non_contiguous=True).then_inc(s_ld.h, 16)
            sy.dma_start(out=dat[0:1, 0:1], in_=da_h[:]).then_inc(s_ld.h, 16)
            sy.dma_start(out=xnb[0:1, 0:D], in_=fb_h[:]).then_inc(s_ld.h, 16)
            sy.dma_start(out=wst[0:DH, 0:FT], in_=wqf_h[:]).then_inc(s_ld.h, 16)
            sy.dma_start(out=wst[0:DH, FT:2 * FT], in_=wkf_h[:]).then_inc(s_ld.h, 16)
        P.emit("sync", s_small)
        P.event("small_ld", s_ld, 16 * 5)

        def v_const(v):
            P.wait(v, "iota")
            for col, val in ((B_EPSN, 1e-5), (B_C2, 1e-11),
                             (B_XE, LN2 * 12.0), (B_XS, -LN2 * 12.0),
                             (B_WE, LN2 * 14.0), (B_WS, -LN2 * 14.0),
                             (B_ONE, 1.0), (B_SPARE, 0.0)):
                v.memset(bias_b[:, col:col + 1], val)
            v.memset(ones1r[:], 1.0)
            v.memset(mxs[:], 0.0)
            v.memset(g2t[:], 0.0)
            v.drain()
            v.memset(g2t[0:64, 0:1], 1.0)
            v.memset(g2t[64:128, 1:2], 1.0)
            v.tensor_scalar(out=eye[:], in0=iof[:], scalar1=iop[:], scalar2=None,
                            op0=ALU.is_equal)
            v.tensor_scalar(out=eyef[:], in0=iof[:], scalar1=iop[:], scalar2=None,
                            op0=ALU.is_equal)
            v.tensor_scalar(out=trilq[:], in0=iof[:], scalar1=iop[:], scalar2=None,
                            op0=ALU.is_ge)
            v.tensor_scalar(out=st_s1[:], in0=iop[:], scalar1=64.0, scalar2=None,
                            op0=ALU.subtract)
            v.drain()
            v.tensor_scalar(out=maskA[:], in0=iof[:], scalar1=st_s1[:],
                            scalar2=None, op0=ALU.is_le)
            v.tensor_scalar(out=maskB[:], in0=iof[:], scalar1=iop[:], scalar2=None,
                            op0=ALU.is_ge)
            v.tensor_scalar(out=st_t[:, 0:1], in0=iop[:], scalar1=64.0, scalar2=None,
                            op0=ALU.add)
            v.drain()
            v.tensor_scalar(out=q_c[:, 0:128], in0=iof[:], scalar1=st_t[:, 0:1],
                            scalar2=None, op0=ALU.is_le)
            v.drain()
            v.tensor_mul(out=maskB[:], in0=maskB[:], in1=q_c[:, 0:128])
            for buf in (psiq, psik):
                bv = buf[:].rearrange("p (h c) -> p h c", c=128)
                v.memset(bv[:, :, 120:121], 1.0)
                v.memset(bv[:, :, 121:128], 0.0)
            v.drain()
            v.sem_inc(s_dv.h, 1)
        P.emit("vector", v_const)
        P.event("consts", s_dv)

        def a_dact(a):
            P.wait(a, "small_ld")
            a.activation(out=dat[0:1, 0:1], in_=dat[0:1, 0:1], func=AF.Tanh, scale=4.0)
            a.drain()
            a.mul(out=dat[0:1, 0:1], in_=dat[0:1, 0:1], mul=0.5)
            a.drain()
            a.copy(out=fbt[:], in_=xnb[0:1, 0:D])
            a.drain()
            a.sem_inc(s_ac.h, 1)
        P.emit("scalar", a_dact)
        P.event("dact0", s_ac)

        def t_bcast(t):
            P.wait(t, "dact0")
            P.wait(t, "consts")
            t.matmul(pf[3][:, 0:1], ones1r[:], dat[0:1, 0:1].bitcast(BF16)[0:1, 0:2],
                     start=True, stop=True).then_inc(s_pe.h, 1)
        def s_dbc(sy):
            P.wait(sy, "dact0")
            sy.dma_start(out=sc1_h, in_=dat[0:1, 0:1]).then_inc(s_cb.h, 16)
        P.emit("sync", s_dbc)
        P.event("dat_d", s_cb, 16)

        def s_dbc2(sy):
            P.wait(sy, "dat_d")
            sy.dma_start(out=dat[:], in_=sc1_h.broadcast_to([128, 1])
                         ).then_inc(s_cb.h, 16)
        P.emit("sync", s_dbc2)
        P.event("dat_bc", s_cb, 16)

        def v_wqfb(v):
            P.wait(v, "small_ld")
            P.wait(v, "consts")
            v.memset(wqfbd[:], 0.0)
            v.drain()
            v.tensor_copy(out=wqfbd[0:DH, 0:FT], in_=wst[0:DH, 0:FT])
            v.tensor_copy(out=wqfbd[DH:128, FT:32], in_=wst[0:DH, 0:FT])
            v.tensor_copy(out=wqfbd[0:DH, 32:48], in_=wst[0:DH, FT:2 * FT])
            v.tensor_copy(out=wqfbd[DH:128, 48:64], in_=wst[0:DH, FT:2 * FT])
            v.drain()
            v.sem_inc(s_dv.h, 1)
        P.emit("vector", v_wqfb)
        P.event("pro0", s_dv)

        def emit_staging():
            wplans = [(wq_h, wqs_h, True, 8), (wk_h, wks_h, True, 8),
                      (wv_h, wvs_h, True, 8), (wo_h, wos_h, False, 16)]
            wtiles = [(src, dst, fold, ci)
                      for (src, dst, fold, nch) in wplans for ci in range(nch)]
            wfold_slots = [wbf2[:, 0:D], wbf2[:, D:2 * D]]
            for i, (src, dst, fold, ci) in enumerate(wtiles):
                if i in (4, 10, 16, 22):
                    phase_A({4: 0, 10: 1, 16: 2, 22: 3}[i])
                lbuf = wst if i % 2 == 0 else oout
                fslot = wfold_slots[i % 2]

                def s_w(sy, src=src, ci=ci, i=i, lbuf=lbuf):
                    if i == 0:
                        P.wait(sy, "pro0")
                    if i >= 2:
                        P.wait(sy, f"wcast_{i - 2}")
                    sy.dma_start(out=lbuf[:], in_=src[ci * 128:(ci + 1) * 128, :]
                                 ).then_inc(s_sg[i % 2].h, 16)
                P.emit("sync", s_w)
                P.event(f"wld_{i}", s_sg[i % 2], 16)

                def v_w(v, fold=fold, ci=ci, i=i, lbuf=lbuf, fslot=fslot):
                    P.wait(v, f"wld_{i}")
                    if i >= 2:
                        P.wait(v, f"wscr_{i - 2}")
                    if fold:
                        v.tensor_scalar(out=fslot, in0=lbuf[:],
                                        scalar1=nwt[:, (ci % 8):(ci % 8) + 1],
                                        scalar2=None, op0=ALU.mult)
                    else:
                        v.tensor_copy(out=fslot, in_=lbuf[:])
                    v.drain()
                    v.sem_inc(s_dv.h, 1)
                P.emit("vector", v_w)
                P.event(f"wcast_{i}", s_dv)

                def a_wd(a, ci=ci, i=i, dst=dst, fslot=fslot):
                    P.wait(a, f"wcast_{i}")
                    a.dma_start(out=dst[ci * 128:(ci + 1) * 128, :],
                                in_=fslot).then_inc(s_sv[i % 4].h, 16)
                P.emit("scalar", a_wd)
                P.event(f"wscr_{i}", s_sv[i % 4], 16)
            P.alias("stg_done", "wscr_39")
            P.alias("oout_free", "wcast_39")
            slot_user[("wbf", 0)] = "pro0"
            slot_user[("wbf", 1)] = "pro0"
            slot_user[("wbf2", 0)] = "wscr_38"
            slot_user[("wbf2", 1)] = "wscr_39"

        def emit_fwq_q(ci):
            def s_fw(sy, ci=ci):
                P.wait(sy, f"fwq_{ci - 1}_e" if ci >= 1 else "stg_done")
                sy.dma_start(out=wst[:], in_=fw_h[ci * 128:(ci + 1) * 128, :]
                             ).then_inc(s_ld.h, 16)
                sy.dma_start(out=q_d[:], in_=dw_h[ci * 128:(ci + 1) * 128, :]
                             ).then_inc(s_ld.h, 16)
            P.emit("sync", s_fw)
            P.event(f"fwld_{ci}", s_ld, 32)
            quant_emit(f"fwq_{ci}", wst[:], -11, 2.0 ** 14,
                       [f"fwld_{ci}", "dat_bc"], q_d[:],
                       reader_ev=f"fwtr_{ci}")

        def emit_fwq_t(ci, chunk):
            par = xq_par[f"fwq_{ci}"]

            def t_fw(t, ci=ci, chunk=chunk, par=par):
                P.wait(t, f"fwq_{ci}")
                if ci >= 1:
                    P.wait(t, f"wqtc_{ci - 1}")
                if f"D_phiev_{chunk}_3" in P.ev:
                    P.wait(t, f"D_phiev_{chunk}_3")
                for j in range(8):
                    t.transpose(out=pbt[:, j * 128:(j + 1) * 128],
                                in_=xqb[:, par * D + j * 128:
                                        par * D + (j + 1) * 128],
                                identity=eye[:])
                t.drain()
                t.sem_inc(s_pe.h, 1)
            P.emit("tensor", t_fw)
            P.event(f"fwtr_{ci}", s_pe)

            def v_fw(v, ci=ci):
                P.wait(v, f"fwtr_{ci}")
                P.wait(v, "B_vmm_0_1_7")
                if ci >= 1:
                    P.wait(v, f"wqt_{ci - 1}")
                v.tensor_copy(out=hT[:, 0:1024], in_=pbt[:])
                v.drain()
                v.sem_inc(s_dv.h, 1)
            P.emit("vector", v_fw)
            P.event(f"wqtc_{ci}", s_dv)

            def a_fwd(a, ci=ci):
                P.wait(a, f"wqtc_{ci}")
                a.dma_start(
                    out=wqt_h[:, ci * 128:(ci + 1) * 128]
                    .rearrange("(j p) c -> p j c", p=128),
                    in_=hT[:, 0:1024].rearrange("p (j c) -> p j c", c=128)
                    ).then_inc(s_vd.h, 16)
            P.emit("scalar", a_fwd)
            P.event(f"wqt_{ci}", s_vd, 16)
            if ci == 7:
                P.alias("prologue", "wqt_7")

        def phase_A(tt, pbt_pred=None):
            par = tt % 2
            xtile = xt[:, par * D:(par + 1) * D]
            lc = tt % 4
            st = tt // 4
            if lc >= 2:
                drip(2)

            def s_x(sy, tt=tt, xtile=xtile):
                if tt >= 2:
                    P.wait(sy, f"A_xtfree_{tt - 2}")
                sy.dma_start(out=xtile, in_=x_h[tt * 128:(tt + 1) * 128, :]
                             ).then_inc(s_x2[tt % 2].h, 16)
            P.emit("sync", s_x)
            P.event(f"A_x_{tt}", s_x2[tt % 2], 16)

            def a_st(a, tt=tt, xtile=xtile):
                P.wait(a, f"A_x_{tt}")
                if tt >= 1:
                    P.wait(a, f"A_xq_{tt - 1}_abs")
                    P.wait(a, f"A_var_{tt - 1}")
                a.activation(out=xnb[:], in_=xtile, func=AF.Square,
                             accum_out=st_s2[:])
                a.drain()
                a.sem_inc(s_ac.h, 1)
            P.emit("scalar", a_st)
            P.event(f"A_sq_{tt}", s_ac)

            def v_st(v, tt=tt, xtile=xtile):
                P.wait(v, f"A_x_{tt}")
                v.tensor_reduce(out=st_s1[:], in_=xtile, axis=AX.X, op=ALU.add)
                v.drain()
                v.tensor_scalar(out=st_m[:], in0=st_s1[:], scalar1=1.0 / D,
                                scalar2=None, op0=ALU.mult)
                P.wait(v, f"A_sq_{tt}")
                v.tensor_scalar(out=st_v[:], in0=st_s2[:], scalar1=1.0 / D,
                                scalar2=None, op0=ALU.mult)
                v.drain()
                v.tensor_mul(out=st_t[:, 0:1], in0=st_m[:], in1=st_m[:])
                v.drain()
                v.tensor_sub(out=st_v[:], in0=st_v[:], in1=st_t[:, 0:1])
                v.drain()
                v.sem_inc(s_dv.h, 1)
            P.emit("vector", v_st)
            P.event(f"A_var_{tt}", s_dv)

            def a_st2(a, tt=tt):
                P.wait(a, f"A_var_{tt}")
                a.activation(out=st_s1[:], in_=st_v[:], func=AF.Sqrt,
                             bias=bias_b[:, B_EPSN:B_EPSN + 1])
                a.activation(out=st_s2[:], in_=st_v[:], func=AF.Sqrt,
                             scale=1.0 + 1e-6, bias=bias_b[:, B_C2:B_C2 + 1])
                a.drain()
                a.sem_inc(s_ac.h, 1)
            P.emit("scalar", a_st2)
            P.event(f"A_sd_{tt}", s_ac)

            def v_st2(v, tt=tt, xtile=xtile):
                P.wait(v, f"A_sd_{tt}")
                if tt >= 1:
                    P.wait(v, f"A_trh_{tt - 1}")
                v.reciprocal(out=st_rs[:], in_=st_s1[:])
                v.reciprocal(out=st_c2[:], in_=st_s2[:])
                v.drain()
                v.tensor_scalar(out=afin[:, tt:tt + 1], in0=st_rs[:], scalar1=1.0,
                                scalar2=None, op0=ALU.add)
                v.tensor_tensor(out=bfin[:, tt:tt + 1], in0=st_m[:], in1=st_rs[:],
                                op=ALU.mult)
                v.drain()
                v.tensor_scalar(out=bfin[:, tt:tt + 1], in0=bfin[:, tt:tt + 1],
                                scalar1=-1.0, scalar2=None, op0=ALU.mult)
                v.tensor_scalar(out=hb[:], in0=xtile, scalar1=st_m[:],
                                scalar2=st_c2[:], op0=ALU.subtract, op1=ALU.mult)
                v.tensor_scalar(out=xnb[:], in0=xtile, scalar1=st_m[:],
                                scalar2=st_rs[:], op0=ALU.subtract, op1=ALU.mult)
                v.drain()
                v.sem_inc(s_dv.h, 1)
            P.emit("vector", v_st2)
            P.event(f"A_hx_{tt}", s_dv)
            P.alias(f"A_xtfree_{tt}", f"A_hx_{tt}")

            quant_emit(f"A_xq_{tt}", xnb[:], -9, 2.0 ** 12,
                       [f"A_hx_{tt}", "dat_bc"], None,
                       reader_ev=f"A_trxmm_{tt}")

            def t_trh(t, tt=tt, st=st, lc=lc, pbt_pred=pbt_pred):
                P.wait(t, f"A_hx_{tt}")
                if tt >= 1:
                    P.wait(t, f"A_hTev_{tt - 1}")
                if pbt_pred is not None and f"D_phiev_{pbt_pred}_3" in P.ev:
                    P.wait(t, f"D_phiev_{pbt_pred}_3")
                if tt == 4 and "wqtc_7" in P.ev:
                    P.wait(t, "wqtc_7")
                for j in range(8):
                    t.transpose(out=pbt[:, j * 128:(j + 1) * 128],
                                in_=hb[:, j * 128:(j + 1) * 128], identity=eye[:])
                t.drain()
                t.sem_inc(s_pe.h, 1)
            P.emit("tensor", t_trh)
            P.event(f"A_trh_{tt}", s_pe)

            def v_trevh(v, tt=tt, lc=lc, st=st):
                dst_h = hT[:].rearrange("p (d c) -> p d c", c=512)
                if st >= 1 and lc == 0:
                    P.wait(v, f"B_vmm_{st - 1}_1_7")
                    if tt == 4 and "wqt_7" in P.ev:
                        P.wait(v, "wqt_7")
                P.wait(v, f"A_trh_{tt}")
                v.tensor_copy(
                    out=dst_h[:, :, lc * 128:(lc + 1) * 128],
                    in_=pbt[:].rearrange("p (j c) -> p j c", c=128))
                v.drain()
                v.sem_inc(s_dv.h, 1)
            P.emit("vector", v_trevh)
            P.event(f"A_hTev_{tt}", s_dv)

        def emit_trx(tt):
            par = xq_par[f"A_xq_{tt}"]
            lc = tt % 4
            st = tt // 4

            def t_trx(t, tt=tt, par=par):
                P.wait(t, f"A_xq_{tt}")
                if tt % 4 == 0:
                    P.wait(t, f"A_hTev_{tt + 3}")
                else:
                    P.wait(t, f"A_trx_{tt - 1}")
                for j in range(8):
                    t.transpose(out=pbt[:, j * 128:(j + 1) * 128],
                                in_=xqb[:, par * D + j * 128:
                                        par * D + (j + 1) * 128],
                                identity=eye[:])
                t.drain()
                t.sem_inc(s_pe.h, 1)
            P.emit("tensor", t_trx)
            P.event(f"A_trxmm_{tt}", s_pe)

            def a_trevx(a, tt=tt, lc=lc, st=st):
                P.wait(a, f"A_trxmm_{tt}")
                if st >= 1 and lc == 0:
                    P.wait(a, f"E_ffmm_{st - 1}_1_7")
                dst_x = xqT[:].rearrange("p (d c) -> p d c", c=512)
                a.activation(out=dst_x[:, :, lc * 128:(lc + 1) * 128],
                             in_=pbt[:].rearrange("p (j c) -> p j c", c=128),
                             func=AF.Copy)
                a.drain()
                a.sem_inc(s_ac.h, 1)
            P.emit("scalar", a_trevx)
            P.event(f"A_trx_{tt}", s_ac)

        def phase_B(st):
            for tagp, wsrc, dstbuf in (("q", wqs_h, qT), ("k", wks_h, kT)):
                for mo in range(8):
                    slot = mo % 2
                    wv = wbf[:, slot * D:(slot + 1) * D]

                    dep = slot_user.get(("wbf", slot), "prologue")

                    def s_wp(sy, mo=mo, wsrc=wsrc, wv=wv, dep=dep, st=st,
                             tagp=tagp):
                        P.wait(sy, dep)
                        if st == 0:
                            P.wait(sy, "wscr_7" if tagp == "q" else "wscr_15")
                        sy.dma_start(
                            out=wv.rearrange("p (a c) -> p a c", a=8),
                            in_=wsrc[:, mo * 128:(mo + 1) * 128]
                            .rearrange("(a p) c -> p a c", p=128)
                        ).then_inc(s_wb[mo % 2].h, 16)
                    P.emit("sync", s_wp)
                    P.event(f"B_wl_{tagp}_{st}_{mo}", s_wb[mo % 2], 16)
                    slot_user[("wbf", slot)] = f"B_{tagp}mm_{st}_{mo}"

                    def t_p(t, st=st, mo=mo, tagp=tagp, wv=wv):
                        bank = mo % 2
                        P.wait(t, f"A_hTev_{st * 4 + 3}")
                        P.wait(t, f"B_wl_{tagp}_{st}_{mo}")
                        if st >= 1 and tagp == "q" and mo < 2:
                            P.wait(t, f"E_res_{st * 4 - 2 + mo}")
                        if mo >= 2:
                            P.wait(t, f"B_{tagp}ev_{st}_{mo - 2}")
                        elif tagp == "k":
                            P.wait(t, f"B_qev_{st}_{6 + mo}")
                        for kc in range(8):
                            t.matmul(pf[bank][:], wv[:, kc * 128:(kc + 1) * 128],
                                     hT[:, kc * 512:(kc + 1) * 512],
                                     start=(kc == 0), stop=(kc == 7))
                        t.drain()
                        t.sem_inc(s_pe.h, 1)
                    P.emit("tensor", t_p)
                    P.event(f"B_{tagp}mm_{st}_{mo}", s_pe)

                    def v_pe(v, st=st, mo=mo, tagp=tagp, dstbuf=dstbuf):
                        bank = mo % 2
                        P.wait(v, f"B_{tagp}mm_{st}_{mo}")
                        if st >= 1 and mo == 0:
                            P.wait(v, f"C_av_{st - 1}_3")
                            P.wait(v, f"D_kv_{st * 4 - 1}")
                        v.tensor_copy(out=dstbuf[:, mo * 512:(mo + 1) * 512],
                                      in_=pf[bank][:])
                        v.drain()
                        v.sem_inc(s_dv.h, 1)
                    P.emit("vector", v_pe)
                    P.event(f"B_{tagp}ev_{st}_{mo}", s_dv)
                    if mo >= 2:
                        drip(1)
                    if mo in (3, 7):
                        emit_trx(st * 4 + (0 if tagp == "q" else 2)
                                 + (0 if mo == 3 else 1))

            for half in range(2):
                for kc in range(8):
                    slot = kc % 2
                    wv2 = wbf2[:, slot * D:slot * D + 512]

                    dep = slot_user.get(("wbf2", slot), "prologue")

                    def s_wv(sy, half=half, kc=kc, wv2=wv2, dep=dep, st=st):
                        P.wait(sy, dep)
                        if st == 0:
                            P.wait(sy, "wscr_23")
                        sy.dma_start(out=wv2,
                                     in_=wvs_h[kc * 128:(kc + 1) * 128,
                                               half * 512:(half + 1) * 512]
                                     ).then_inc(s_v2[kc % 2].h, 16)
                    P.emit("sync", s_wv)
                    P.event(f"B_wlv_{st}_{half}_{kc}", s_v2[kc % 2], 16)
                    slot_user[("wbf2", slot)] = f"B_vmm_{st}_{half}_{kc}"

                    def t_v(t, st=st, half=half, kc=kc, wv2=wv2):
                        P.wait(t, f"B_wlv_{st}_{half}_{kc}")
                        if kc == 0 and half == 0:
                            P.wait(t, f"A_hTev_{st * 4 + 3}")
                            if st >= 1:
                                P.wait(t, f"E_res_{st * 4 - 1}")
                        if kc == 0:
                            pr = f"B_vev_{st}_{half - 1}_3" if half == 1 else None
                            if pr and pr in P.ev:
                                P.wait(t, pr)
                            if half == 0:
                                P.wait(t, f"B_kev_{st}_6")
                                P.wait(t, f"B_kev_{st}_7")
                        for lc in range(4):
                            t.matmul(pf[lc][:],
                                     hT[:, kc * 512 + lc * 128:kc * 512 + (lc + 1) * 128],
                                     wv2, start=(kc == 0), stop=(kc == 7))
                        t.drain()
                        t.sem_inc(s_pe.h, 1)
                    P.emit("tensor", t_v)
                    P.event(f"B_vmm_{st}_{half}_{kc}", s_pe)

                for lc in range(4):
                    def v_vev(v, st=st, half=half, lc=lc):
                        P.wait(v, f"B_vmm_{st}_{half}_7")
                        if st >= 1 and half == 0:
                            P.wait(v, f"D_kv_{st * 4 - 1}")
                            P.wait(v, f"C_av_{st - 1}_3")
                        dst = v65[:, lc * VW:(lc + 1) * VW] \
                            .rearrange("p (h c) -> p h c", c=65)[:, half * 8:half * 8 + 8, 0:64]
                        v.tensor_copy(out=dst,
                                      in_=pf[lc][:].rearrange("p (h c) -> p h c", c=64))
                        if half == 1:
                            ones = v65[:, lc * VW:(lc + 1) * VW] \
                                .rearrange("p (h c) -> p h c", c=65)[:, :, 64:65]
                            v.memset(ones, 1.0)
                        v.drain()
                        v.sem_inc(s_dv.h, 1)
                    P.emit("vector", v_vev)
                    P.event(f"B_vev_{st}_{half}_{lc}", s_dv)
                    drip(1)

            for si, (tsrc, fdst, colofs) in enumerate(((qT, qftm, 0), (kT, kftm, 32))):
                for lc in range(4):
                    bank = lc % 2

                    def t_qf(t, st=st, si=si, lc=lc, tsrc=tsrc, colofs=colofs):
                        bank = lc % 2
                        P.wait(t, f"B_{'q' if si == 0 else 'k'}ev_{st}_7")
                        P.wait(t, "pro0")
                        if si == 0 and lc == 0:
                            P.wait(t, f"B_vev_{st}_1_3")
                        pr = f"B_fev_{st}_{si}_{lc - 2}"
                        if pr in P.ev:
                            P.wait(t, pr)
                        elif si == 1:
                            pr2 = f"B_fev_{st}_0_{2 + lc}"
                            if pr2 in P.ev:
                                P.wait(t, pr2)
                        for pair in range(8):
                            t.matmul(pf[bank][:, pair * 32:(pair + 1) * 32],
                                     tsrc[:, pair * 512 + lc * 128:pair * 512 + (lc + 1) * 128],
                                     wqfbd[:, colofs:colofs + 32],
                                     start=True, stop=True)
                        t.drain()
                        t.sem_inc(s_pe.h, 1)
                    P.emit("tensor", t_qf)
                    P.event(f"B_fmm_{st}_{si}_{lc}", s_pe)

                    def v_fev(v, st=st, si=si, lc=lc, fdst=fdst):
                        bank = lc % 2
                        P.wait(v, f"B_fmm_{st}_{si}_{lc}")
                        if st >= 1 and si == 0 and lc == 0:
                            P.wait(v, f"D_kv_{st * 4 - 1}")
                        src = pf[bank][:, 0:256].rearrange("p (a e i) -> p a e i",
                                                           e=2, i=16)
                        dst = fdst[:, lc * 512:(lc + 1) * 512] \
                            .rearrange("p (a e s i) -> p a e s i", e=2, s=2, i=16)
                        v.tensor_copy(out=dst[:, :, :, 0, :], in_=src)
                        v.drain()
                        sqv = fdst[:, lc * 512:(lc + 1) * 512] \
                            .rearrange("p (hh s i) -> p hh s i", s=2, i=16)
                        v.scalar_tensor_tensor(out=sqv[:, :, 1, :],
                                               in0=sqv[:, :, 0, :],
                                               scalar=(0.25 if si == 1 else 1.0),
                                               in1=sqv[:, :, 0, :],
                                               op0=ALU.mult, op1=ALU.mult)
                        v.drain()
                        v.sem_inc(s_dv.h, 1)
                    P.emit("vector", v_fev)
                    P.event(f"B_fev_{st}_{si}_{lc}", s_dv)
                    drip(1)

            for si, tsrc in ((0, qT), (1, kT)):
                for pair in range(8):
                    gi = si * 8 + pair
                    sqb = sqs if gi % 2 == 0 else sqs2
                    mxb = pf[3] if gi % 2 == 0 else pf[2]
                    stc = gi % 2

                    def pr2(gi=gi, st=st):
                        g2 = gi - 2
                        if g2 < 0:
                            return None
                        return f"B_mx_{st}_{g2 // 8}_{g2 % 8}"

                    def v_sq(v, st=st, si=si, pair=pair, tsrc=tsrc, sqb=sqb,
                             pr2=pr2):
                        P.wait(v, f"B_{'q' if si == 0 else 'k'}ev_{st}_7")
                        pr = pr2()
                        if pr and pr in P.ev:
                            P.wait(v, pr)
                        v.tensor_mul(out=sqb[:],
                                     in0=tsrc[:, pair * 512:(pair + 1) * 512],
                                     in1=tsrc[:, pair * 512:(pair + 1) * 512])
                        v.drain()
                        v.sem_inc(s_dv.h, 1)
                    P.emit("vector", v_sq)
                    P.event(f"B_sqs_{st}_{si}_{pair}", s_dv)

                    def t_sq1(t, st=st, si=si, pair=pair, gi=gi, sqb=sqb,
                              mxb=mxb, pr2=pr2):
                        P.wait(t, f"B_sqs_{st}_{si}_{pair}")
                        if gi == 0:
                            P.wait(t, f"B_vev_{st}_1_3")
                        if gi == 1:
                            P.wait(t, f"B_vev_{st}_1_2")
                        pr = pr2()
                        if pr and pr in P.ev:
                            P.wait(t, pr)
                        t.matmul(mxb[0:2, 0:512], g2t[:], sqb[:],
                                 start=True, stop=True)
                        t.drain()
                        t.sem_inc(s_pe.h, 1)
                    P.emit("tensor", t_sq1)
                    P.event(f"B_sqmm_{st}_{si}_{pair}", s_pe)

                    def v_mx(v, st=st, si=si, pair=pair, mxb=mxb, stc=stc):
                        P.wait(v, f"B_sqmm_{st}_{si}_{pair}")
                        v.tensor_reduce(out=st_t[0:2, stc:stc + 1],
                                        in_=mxb[0:2, 0:512],
                                        axis=AX.X, op=ALU.max)
                        v.drain()
                        col = si * 9 + pair
                        v.tensor_tensor(out=mxs[0:2, col:col + 1],
                                        in0=mxs[0:2, col:col + 1],
                                        in1=st_t[0:2, stc:stc + 1], op=ALU.max)
                        v.drain()
                        v.sem_inc(s_dv.h, 1)
                    P.emit("vector", v_mx)
                    P.event(f"B_mx_{st}_{si}_{pair}", s_dv)
                    drip(1)

            def v_cb1(v, st=st):
                P.wait(v, f"B_mx_{st}_1_7")
                v.tensor_reduce(out=mxs[0:2, 8:9], in_=mxs[0:2, 0:8], axis=AX.X,
                                op=ALU.max)
                v.tensor_reduce(out=mxs[0:2, 17:18], in_=mxs[0:2, 9:17], axis=AX.X,
                                op=ALU.max)
                v.drain()
                v.tensor_add(out=mxs[0:2, 18:19], in0=mxs[0:2, 8:9],
                             in1=mxs[0:2, 17:18])
                v.drain()
                v.sem_inc(s_dv.h, 1)
            P.emit("vector", v_cb1)
            P.event(f"B_cb1_{st}", s_dv)

            def t_cb(t, st=st):
                P.wait(t, f"B_cb1_{st}")
                P.wait(t, f"B_vev_{st}_1_2")
                t.transpose(out=pf[2][0:1, 0:2], in_=mxs[0:2, 18:19],
                            identity=eyef[0:2, 0:2])
                t.drain()
                t.sem_inc(s_pe.h, 1)
            P.emit("tensor", t_cb)
            P.event(f"B_cbt_{st}", s_pe)

            def v_cb2(v, st=st):
                P.wait(v, f"B_cbt_{st}")
                v.tensor_reduce(out=st_t[0:1, 0:1], in_=pf[2][0:1, 0:2], axis=AX.X,
                                op=ALU.max)
                v.drain()
                v.tensor_scalar(out=st_t[0:1, 0:1], in0=st_t[0:1, 0:1],
                                scalar1=-1.0 / 16.0, scalar2=10.0 * LN2,
                                op0=ALU.mult, op1=ALU.add)
                v.drain()
                v.sem_inc(s_dv.h, 1)
            P.emit("vector", v_cb2)
            P.event(f"B_cb2_{st}", s_dv)

            def a_cbd(a, st=st):
                P.wait(a, f"B_cb2_{st}")
                a.dma_start(out=sc2_h, in_=st_t[0:1, 0:1]).then_inc(s_cb.h, 16)
            P.emit("scalar", a_cbd)
            P.event(f"B_cbd_{st}", s_cb, 16)

            def s_cbb(sy, st=st):
                P.wait(sy, f"B_cbd_{st}")
                sy.dma_start(out=cbias[:], in_=sc2_h.broadcast_to([128, 1])
                             ).then_inc(s_cb.h, 16)
            P.emit("sync", s_cbb)
            P.event(f"B_cbias_{st}", s_cb, 16)

        def emit_psi(si, st, lc):
            c = st * 4 + lc
            fsrc, psi = ((qftm, psiq), (kftm, psik))[si]

            def v_psi(v, st=st, lc=lc, si=si, fsrc=fsrc, psi=psi, c=c):
                P.wait(v, f"B_fev_{st}_{si}_{lc}")
                dep = f"D_kv_{c - 1}" if si == 1 else f"D_phitr_{c - 1}"
                if c >= 1 and dep in P.ev:
                    P.wait(v, dep)
                base = lc * 512
                qv = fsrc[:, base:base + 512].rearrange(
                    "p (hh s i) -> p hh s i", s=2, i=16)
                pv = psi[:].rearrange("p (h c) -> p h c", c=128)
                off = 0
                for dlt in range(1, 16):
                    n = 16 - dlt
                    if si == 0:
                        v.tensor_tensor(out=pv[:, :, off:off + n],
                                        in0=qv[:, :, 0, 0:n],
                                        in1=qv[:, :, 0, dlt:dlt + n],
                                        op=ALU.mult)
                    else:
                        v.scalar_tensor_tensor(out=pv[:, :, off:off + n],
                                               in0=qv[:, :, 0, 0:n], scalar=0.5,
                                               in1=qv[:, :, 0, dlt:dlt + n],
                                               op0=ALU.mult, op1=ALU.mult)
                    off += n
                v.drain()
                v.sem_inc(s_dv.h, 1)
            P.emit("vector", v_psi)
            P.event(f"D_psi{si}_{c}", s_dv)

        def phase_C(st, lc):
            import os as _os
            _csub = int(_os.environ.get("KMK_CSUB", "99"))
            if lc >= int(_os.environ.get("KMK_CLC", "4")):
                return
            c = st * 4 + lc
            first = (c == 0)

            emit_psi(0, st, lc)

            import os as _os2
            for g in range(int(_os2.environ.get("KMK_CG", "4"))):
                def t_sc(t, st=st, lc=lc, g=g, first=first, c=c):
                    if g == 0:
                        import os as _os3
                        if _os3.environ.get("KMK_NOCB") != "1":
                            P.wait(t, f"B_cbias_{st}")
                        P.wait(t, f"B_vev_{st}_1_{lc}")
                        if lc == 0 and st >= 1:
                            P.wait(t, f"C_halo_{st - 1}")
                        if f"D_zl_{c - 1}" in P.ev:
                            P.wait(t, f"D_zl_{c - 1}")
                        if f"D_kva_{c - 1}" in P.ev:
                            P.wait(t, f"D_kva_{c - 1}")
                        if lc == 0:
                            P.wait(t, f"B_cb2_{st}")
                    pr = f"C_expB_{st}_{lc}_{g - 1}"
                    if pr in P.ev:
                        P.wait(t, pr)
                    elif not first and f"C_expA_{st}_{lc - 1 if lc else 0}_x" in P.ev:
                        pass
                    for hh in (0, 2, 1, 3):
                        h = g * 4 + hh
                        pair, e = h // 2, h % 2
                        t.matmul(pf[0][:, hh * 128:(hh + 1) * 128],
                                 kT[e * 64:(e + 1) * 64,
                                    pair * 512 + lc * 128:pair * 512 + (lc + 1) * 128],
                                 qT[e * 64:(e + 1) * 64,
                                    pair * 512 + lc * 128:pair * 512 + (lc + 1) * 128],
                                 start=True, stop=True)
                    t.drain()
                    t.sem_inc(s_pe.h, 1)
                    if not first:
                        pr = f"C_expA_{st}_{lc}_{g - 1}"
                        if pr in P.ev:
                            P.wait(t, pr)
                        for hh in (1, 3, 0, 2):
                            h = g * 4 + hh
                            pair, e = h // 2, h % 2
                            if lc == 0:
                                ksl = kThal[e * 64:(e + 1) * 64,
                                            pair * 128 + 64:pair * 128 + 128]
                            else:
                                ksl = kT[e * 64:(e + 1) * 64,
                                         pair * 512 + lc * 128 - 64:pair * 512 + lc * 128]
                            t.matmul(pf[1][64:128, hh * 128:(hh + 1) * 128], ksl,
                                     qT[e * 64:(e + 1) * 64,
                                        pair * 512 + lc * 128:pair * 512 + (lc + 1) * 128],
                                     start=True, stop=True)
                        t.drain()
                        t.sem_inc(s_pe.h, 1)
                P.emit("tensor", t_sc)
                P.event(f"C_scB_{st}_{lc}_{g}", s_pe)
                if not first:
                    P.event(f"C_scA_{st}_{lc}_{g}", s_pe)

                if _csub < 2:
                    continue
                def a_exp(a, st=st, lc=lc, g=g, first=first):
                    P.wait(a, f"C_scB_{st}_{lc}_{g}")
                    a.activation(out=attB[:, g * 512:(g + 1) * 512],
                                 in_=pf[0][:, 0:512], func=AF.Exp, scale=0.125,
                                 bias=cbias[:, 0:1])
                    a.drain()
                    a.sem_inc(s_ac.h, 1)
                    if not first:
                        P.wait(a, f"C_scA_{st}_{lc}_{g}")
                        a.activation(out=attA[64:128, g * 512:(g + 1) * 512],
                                     in_=pf[1][64:128, 0:512], func=AF.Exp, scale=0.125,
                                     bias=cbias[64:128, 0:1])
                        a.drain()
                        a.sem_inc(s_ac.h, 1)
                P.emit("scalar", a_exp)
                P.event(f"C_expB_{st}_{lc}_{g}", s_ac)
                if not first:
                    P.event(f"C_expA_{st}_{lc}_{g}", s_ac)

                if _csub < 3:
                    continue
                def v_msk(v, st=st, lc=lc, g=g, first=first):
                    P.wait(v, f"C_expB_{st}_{lc}_{g}")
                    bm = attB[:, g * 512:(g + 1) * 512].rearrange(
                        "p (hh c) -> p hh c", c=128)
                    v.tensor_tensor(out=bm, in0=bm,
                                    in1=maskB[:, None, :].broadcast_to([128, 4, 128]),
                                    op=ALU.mult)
                    v.drain()
                    v.sem_inc(s_dv.h, 1)
                    if not first:
                        P.wait(v, f"C_expA_{st}_{lc}_{g}")
                        am = attA[64:128, g * 512:(g + 1) * 512].rearrange(
                            "p (hh c) -> p hh c", c=128)
                        v.tensor_tensor(out=am, in0=am,
                                        in1=maskA[64:128, None, :]
                                        .broadcast_to([64, 4, 128]),
                                        op=ALU.mult)
                        v.drain()
                        v.sem_inc(s_dv.h, 1)
                P.emit("vector", v_msk)
                P.event(f"C_mskB_{st}_{lc}_{g}", s_dv)
                drip(1)
                if not first:
                    P.event(f"C_mskA_{st}_{lc}_{g}", s_dv)

            if _csub < 4:
                return
            def t_av(t, st=st, lc=lc, first=first):
                P.wait(t, f"C_expB_{st}_{lc}_3")
                for h in range(H):
                    g = h // 4
                    P.wait(t, f"C_mskB_{st}_{lc}_{g}")
                    if not first:
                        P.wait(t, f"C_mskA_{st}_{lc}_{g}")
                    if h < 7:
                        dst = pf[2][:, h * 65:(h + 1) * 65]
                    elif h < 14:
                        dst = pf[3][:, (h - 7) * 65:(h - 6) * 65]
                    else:
                        dst = pf[0][:, (h - 14) * 65:(h - 13) * 65]
                    vsrc = v65[:, lc * VW + h * 65:lc * VW + (h + 1) * 65]
                    if first:
                        t.matmul(dst, attB[:, h * 128:(h + 1) * 128], vsrc,
                                 start=True, stop=True)
                    else:
                        vprev = (v65hal[:, h * 65:(h + 1) * 65] if lc == 0 else
                                 v65[:, (lc - 1) * VW + h * 65:(lc - 1) * VW + (h + 1) * 65])
                        t.matmul(dst, attA[64:128, h * 128:(h + 1) * 128],
                                 vprev[64:128, :], start=True, stop=False)
                        t.matmul(dst, attB[:, h * 128:(h + 1) * 128], vsrc,
                                 start=False, stop=True)
                t.drain()
                t.sem_inc(s_pe.h, 1)
            P.emit("tensor", t_av)
            P.event(f"C_av_{st}_{lc}", s_pe)

            if _csub < 5:
                return
            def v_zw(v, st=st, lc=lc):
                P.wait(v, f"C_av_{st}_{lc}")
                if st >= 1:
                    P.wait(v, f"E_trE_{(st - 1) * 4 + lc}_1")
                for bi, (bank, hs, nh) in enumerate(((pf[2], 0, 7), (pf[3], 7, 7),
                                                     (pf[0], 14, 2))):
                    zsl = bank[:, 0:65 * nh].rearrange("p (h c) -> p h c", c=65)
                    v.tensor_copy(out=zs[:, hs:hs + nh, None], in_=zsl[:, :, 64:65])
                v.drain()
                v.reciprocal(out=rz[:], in_=zs[:])
                v.drain()
                for bi, (bank, hs, nh) in enumerate(((pf[2], 0, 7), (pf[3], 7, 7),
                                                     (pf[0], 14, 2))):
                    ysl = bank[:, 0:65 * nh].rearrange("p (h c) -> p h c", c=65)
                    dst = win_tm[:, lc * D + hs * 64:lc * D + (hs + nh) * 64] \
                        .rearrange("p (h c) -> p h c", c=64)
                    v.tensor_tensor(out=dst, in0=ysl[:, :, 0:64],
                                    in1=rz[:, hs:hs + nh, None]
                                    .broadcast_to([128, nh, 64]), op=ALU.mult)
                v.drain()
                v.sem_inc(s_dv.h, 1)
            P.emit("vector", v_zw)
            P.event(f"C_zw_{st}_{lc}", s_dv)
            drip(1)

        def emit_halo(st):
            def v_halo(v, st=st):
                P.wait(v, f"C_zw_{st}_3")
                if f"D_kv_{st * 4 + 3}" in P.ev:
                    P.wait(v, f"D_kv_{st * 4 + 3}")
                src = kT[:].rearrange("p (a c) -> p a c", c=512)[:, :, 384:512]
                v.tensor_copy(out=kThal[:].rearrange("p (a c) -> p a c", c=128),
                              in_=src)
                v.tensor_copy(out=v65hal[:], in_=v65[:, 3 * VW:4 * VW])
                v.drain()
                v.sem_inc(s_dv.h, 1)
            P.emit("vector", v_halo)
            P.event(f"C_halo_{st}", s_dv)

        def phase_D(st, lc):
            import os as _os
            _dsub = int(_os.environ.get("KMK_DSUB", "99"))
            c = st * 4 + lc
            first = (c == 0)
            last = (c == NT - 1)

            emit_psi(1, st, lc)

            if _dsub < 2:
                return
            def t_trD(t, st=st, lc=lc, c=c):
                P.wait(t, f"B_fev_{st}_0_{lc}")
                P.wait(t, f"B_fev_{st}_1_{lc}")
                if lc == 0 and f"A_trx_{st * 4 + 3}" in P.ev:
                    P.wait(t, f"A_trx_{st * 4 + 3}")
                if st == 0 and lc >= 1 and f"wqtc_{2 * lc - 1}" in P.ev:
                    P.wait(t, f"wqtc_{2 * lc - 1}")
                if st >= 1 and lc >= 1 \
                        and f"A_hTev_{(st + 1) * 4 + lc - 1}" in P.ev:
                    P.wait(t, f"A_hTev_{(st + 1) * 4 + lc - 1}")
                P.wait(t, f"C_zw_{st}_{lc}")
                if f"D_zl_{c - 1}" in P.ev:
                    P.wait(t, f"D_zl_{c - 1}")
                elif f"D_smm_{c - 1}_3" in P.ev:
                    P.wait(t, f"D_smm_{c - 1}_3")
                if f"C_expA_{st}_{lc}_3" in P.ev:
                    P.wait(t, f"C_expA_{st}_{lc}_3")
                for j in range(8):
                    t.transpose(out=pbt[0:64, j * 128:(j + 1) * 128],
                                in_=qftm[:, lc * 512 + j * 64:lc * 512 + (j + 1) * 64],
                                identity=eye[:])
                t.drain()
                t.sem_inc(s_pe.h, 1)
                P.wait(t, f"D_fmev_0_{c}")
                for j in range(8):
                    t.transpose(out=pbt[0:64, j * 128:(j + 1) * 128],
                                in_=kftm[:, lc * 512 + j * 64:lc * 512 + (j + 1) * 64],
                                identity=eye[:])
                t.drain()
                t.sem_inc(s_pe.h, 1)
            P.emit("tensor", t_trD)
            P.event(f"D_qftr_{c}", s_pe)
            P.event(f"D_kftr_{c}", s_pe)

            for si in range(2):
                def v_fmev(v, si=si, c=c):
                    P.wait(v, f"D_{'qf' if si == 0 else 'kf'}tr_{c}")
                    v.tensor_copy(out=(qfFM if si == 0 else kfFM)[:],
                                  in_=pbt[0:64, :])
                    v.drain()
                    v.sem_inc(s_dv.h, 1)
                P.emit("vector", v_fmev)
                P.event(f"D_fmev_{si}_{c}", s_dv)
                drip(1)

            if _dsub < 3:
                return
            def t_phitr(t, c=c):
                P.wait(t, f"D_psi0_{c}")
                P.wait(t, f"D_fmev_1_{c}")
                for r in range(4):
                    for j in range(4):
                        hh = r * 4 + j
                        t.transpose(out=pbt[:, j * 128:(j + 1) * 128],
                                    in_=psiq[:, hh * 128:(hh + 1) * 128],
                                    identity=eye[:])
                    t.drain()
                    t.sem_inc(s_pe.h, 1)
                    P.wait(t, f"D_phiev_{c}_{r}")
            P.emit("tensor", t_phitr)
            for r in range(4):
                P.event(f"D_phitr_{c}_{r}", s_pe)

                def v_phiev(v, c=c, r=r):
                    P.wait(v, f"D_phitr_{c}_{r}")
                    v.tensor_copy(out=phiq[:, r * 512:(r + 1) * 512],
                                  in_=pbt[:, 0:512])
                    v.drain()
                    v.sem_inc(s_dv.h, 1)
                P.emit("vector", v_phiev)
                P.event(f"D_phiev_{c}_{r}", s_dv)
                drip(1)
            P.alias(f"D_phitr_{c}", f"D_phiev_{c}_3")

            if _dsub < 4:
                return
            for g in range(4):
                sbank = pf[1] if g % 2 == 0 else pf[0]

                def t_s(t, c=c, g=g, st=st, lc=lc, sbank=sbank):
                    P.wait(t, f"D_fmev_0_{c}")
                    P.wait(t, f"D_fmev_1_{c}")
                    if g == 0:
                        if f"C_expA_{st}_{lc}_3" in P.ev:
                            P.wait(t, f"C_expA_{st}_{lc}_3")
                    if g == 1:
                        P.wait(t, f"C_expB_{st}_{lc}_3")
                    pr = f"D_sq_{c}_{g - 2}"
                    if pr in P.ev:
                        P.wait(t, pr)
                    for hh in ((0, 2, 1, 3) if g % 2 == 0 else (1, 3, 0, 2)):
                        h = g * 4 + hh
                        j, e = h // 2, h % 2
                        t.matmul(sbank[:, hh * 128:(hh + 1) * 128],
                                 kfFM[32 * e:32 * e + 16, j * 128:(j + 1) * 128],
                                 qfFM[32 * e:32 * e + 16, j * 128:(j + 1) * 128],
                                 start=True, stop=True)
                    t.drain()
                    t.sem_inc(s_pe.h, 1)
                P.emit("tensor", t_s)
                P.event(f"D_smm_{c}_{g}", s_pe)

                def a_sq(a, c=c, g=g, sbank=sbank):
                    P.wait(a, f"D_smm_{c}_{g}")
                    a.activation(out=atil[:, g * 512:(g + 1) * 512],
                                 in_=sbank[:, 0:512], func=AF.Square, scale=0.5,
                                 bias=bias_b[:, B_ONE:B_ONE + 1])
                    a.drain()
                    a.sem_inc(s_ac.h, 1)
                P.emit("scalar", a_sq)
                P.event(f"D_sq_{c}_{g}", s_ac)

                def v_tril(v, c=c, g=g):
                    P.wait(v, f"D_sq_{c}_{g}")
                    if c >= 1:
                        P.wait(v, f"D_ymm_{c - 1}")
                    am = atil[:, g * 512:(g + 1) * 512].rearrange(
                        "p (hh x) -> p hh x", x=128)
                    v.tensor_tensor(out=am, in0=am,
                                    in1=trilq[:, None, :].broadcast_to([128, 4, 128]),
                                    op=ALU.mult)
                    v.drain()
                    v.sem_inc(s_dv.h, 1)
                P.emit("vector", v_tril)
                P.event(f"D_tril_{c}_{g}", s_dv)
                drip(1)


            if _dsub < 5:
                return
            def t_y(t, c=c, st=st, lc=lc, first=first):
                if f"D_phiev_{c}_3" in P.ev:
                    P.wait(t, f"D_phiev_{c}_3")
                for g in range(4):
                    P.wait(t, f"D_tril_{c}_{g}")
                if (not first) and f"D_kvev_{c - 1}" in P.ev:
                    P.wait(t, f"D_kvev_{c - 1}")
                P.wait(t, f"C_zw_{st}_{lc}")
                for h in range(H):
                    if h < 7:
                        dst = pf[2][:, h * 65:(h + 1) * 65]
                    elif h < 14:
                        dst = pf[3][:, (h - 7) * 65:(h - 6) * 65]
                    else:
                        dst = pf[1][:, (h - 14) * 65:(h - 13) * 65]
                    vsrc = v65[:, lc * VW + h * 65:lc * VW + (h + 1) * 65]
                    t.matmul(dst, atil[:, h * 128:(h + 1) * 128], vsrc,
                             start=True, stop=first)
                    if not first:
                        t.matmul(dst, phiq[:, h * 128:(h + 1) * 128],
                                 kv_ab[:, h * 65:(h + 1) * 65], start=False,
                                 stop=False)
                        t.matmul(dst, qfFM[32 * (h % 2):32 * (h % 2) + 32,
                                           (h // 2) * 128:(h // 2 + 1) * 128],
                                 kv_bb[32 * (h % 2):32 * (h % 2) + 32,
                                       (h // 2) * 65:(h // 2 + 1) * 65],
                                 start=False, stop=True)
                t.drain()
                t.sem_inc(s_pe.h, 1)
            P.emit("tensor", t_y)
            P.event(f"D_ymm_{c}", s_pe)

            if _dsub < 6:
                return
            def v_zl(v, c=c, lc=lc, st=st):
                P.wait(v, f"D_ymm_{c}")
                if st >= 1:
                    P.wait(v, f"E_trE_{(st - 1) * 4 + lc}_1")
                for bi, (bank, hs, nh) in enumerate(((pf[2], 0, 7), (pf[3], 7, 7),
                                                     (pf[1], 14, 2))):
                    zsl = bank[:, 0:65 * nh].rearrange("p (h x) -> p h x", x=65)
                    v.tensor_copy(out=zs[:, hs:hs + nh, None], in_=zsl[:, :, 64:65])
                v.drain()
                v.tensor_scalar(out=zs[:], in0=zs[:], scalar1=1e-6, scalar2=None,
                                op0=ALU.add)
                v.drain()
                v.reciprocal(out=rz[:], in_=zs[:])
                v.drain()
                for bi, (bank, hs, nh) in enumerate(((pf[2], 0, 7), (pf[3], 7, 7),
                                                     (pf[1], 14, 2))):
                    ysl = bank[:, 0:65 * nh].rearrange("p (h x) -> p h x", x=65)
                    dst = lin_tm[:, lc * D + hs * 64:lc * D + (hs + nh) * 64] \
                        .rearrange("p (h x) -> p h x", x=64)
                    v.tensor_tensor(out=dst, in0=ysl[:, :, 0:64],
                                    in1=rz[:, hs:hs + nh, None]
                                    .broadcast_to([128, nh, 64]), op=ALU.mult)
                v.drain()
                v.sem_inc(s_dv.h, 1)
            P.emit("vector", v_zl)
            P.event(f"D_zl_{c}", s_dv)
            drip(1)

            if _dsub < 7:
                return
            def t_kv(t, c=c, lc=lc, first=first):
                P.wait(t, f"D_psi1_{c}")
                P.wait(t, f"D_sq_{c}_3")
                if (not first) and f"D_kvev_{c - 1}" in P.ev:
                    P.wait(t, f"D_kvev_{c - 1}")
                def vsl(h):
                    return v65[:, lc * VW + h * 65:lc * VW + (h + 1) * 65]

                def kfl(h):
                    return kftm[:, lc * 512 + h * 32:lc * 512 + h * 32 + 32]

                t.matmul(pf[0][:, 0:65], psik[:, 14 * 128:15 * 128], vsl(14),
                         start=True, stop=False)
                t.matmul(pf[0][0:32, 130:195], kfl(14), vsl(14),
                         start=False, stop=False, skip_group_check=True)
                t.matmul(pf[0][32:64, 130:195], kfl(15), vsl(15),
                         start=False, stop=False, skip_group_check=True)
                t.matmul(pf[0][:, 65:130], psik[:, 15 * 128:16 * 128], vsl(15),
                         start=False, stop=True)
                t.drain()
                t.sem_inc(s_pe.h, 1)
                for h in range(14):
                    if h < 7:
                        dst = kva_p[:, h * 65:(h + 1) * 65]
                        st_, sp_ = (h == 0), (h == 6)
                    else:
                        dst = kva_p2[:, (h - 7) * 65:(h - 6) * 65]
                        st_, sp_ = (h == 7), (h == 13)
                    t.matmul(dst, psik[:, h * 128:(h + 1) * 128], vsl(h),
                             start=st_, stop=sp_)
                    dstb = kvmix[32 * (h % 2):32 * (h % 2) + 32,
                                 (h // 2) * 65:(h // 2 + 1) * 65]
                    t.matmul(dstb, kfl(h), vsl(h),
                             start=(h <= 1), stop=(h >= 12),
                             skip_group_check=True)
                t.drain()
                t.sem_inc(s_pe.h, 1)
            P.emit("tensor", t_kv)
            P.event(f"D_kvp0_{c}", s_pe)
            P.event(f"D_kv_{c}", s_pe)

            def v_kvev_a(v, c=c, first=first):
                P.wait(v, f"D_kvp0_{c}")
                if first:
                    v.tensor_copy(out=kva_sb[:, 910:1040], in_=pf[0][:, 0:130])
                    v.tensor_copy(out=kvb_sb[:, 455:520], in_=pf[0][0:64, 130:195])
                else:
                    v.tensor_add(out=kva_sb[:, 910:1040], in0=kva_sb[:, 910:1040],
                                 in1=pf[0][:, 0:130])
                    v.tensor_add(out=kvb_sb[:, 455:520], in0=kvb_sb[:, 455:520],
                                 in1=pf[0][0:64, 130:195])
                v.drain()
                v.tensor_copy(out=kv_ab[:, 910:1040], in_=kva_sb[:, 910:1040])
                v.tensor_copy(out=kv_bb[:, 455:520], in_=kvb_sb[:, 455:520])
                v.drain()
                v.sem_inc(s_dv.h, 1)
            P.emit("vector", v_kvev_a)
            P.event(f"D_kva_{c}", s_dv)
            drip(1)

            def v_kvev(v, c=c, first=first):
                P.wait(v, f"D_kv_{c}")
                if first:
                    v.tensor_copy(out=kva_sb[:, 0:455], in_=kva_p[:])
                    v.tensor_copy(out=kva_sb[:, 455:910], in_=kva_p2[:])
                    v.tensor_copy(out=kvb_sb[:, 0:455], in_=kvmix[0:64, 0:455])
                else:
                    v.tensor_add(out=kva_sb[:, 0:455], in0=kva_sb[:, 0:455],
                                 in1=kva_p[:])
                    v.tensor_add(out=kva_sb[:, 455:910], in0=kva_sb[:, 455:910],
                                 in1=kva_p2[:])
                    v.tensor_add(out=kvb_sb[:, 0:455], in0=kvb_sb[:, 0:455],
                                 in1=kvmix[0:64, 0:455])
                v.drain()
                v.tensor_copy(out=kv_ab[:, 0:910], in_=kva_sb[:, 0:910])
                v.tensor_copy(out=kv_bb[:, 0:455], in_=kvb_sb[:, 0:455])
                v.drain()
                v.sem_inc(s_dv.h, 1)
            P.emit("vector", v_kvev)
            P.event(f"D_kvev_{c}", s_dv)
            drip(1)

        def phase_E(st):
            for pr_i in range(2):
                tiles = (pr_i * 2, pr_i * 2 + 1)
                for ti, tloc in enumerate(tiles):
                    tt = st * 4 + tloc
                    slot = ti

                    def t_trE(t, st=st, tloc=tloc, slot=slot, tt=tt, pr_i=pr_i):
                        P.wait(t, f"D_zl_{st * 4 + tloc}")
                        P.wait(t, f"C_zw_{st}_{tloc}")
                        if st == 0 and pr_i == 0 and slot == 0 \
                                and "wqtc_7" in P.ev:
                            P.wait(t, "wqtc_7")
                        if pr_i == 0 and slot == 0 \
                                and f"A_hTev_{st * 4 + 7}" in P.ev:
                            P.wait(t, f"A_hTev_{st * 4 + 7}")
                        for r in range(2):
                            for j in range(8):
                                fj = r * 8 + j
                                srcb = (lin_tm if fj < 8 else win_tm)
                                jj = fj % 8
                                t.transpose(out=pbt[:, j * 128:(j + 1) * 128],
                                            in_=srcb[:, tloc * D + jj * 128:
                                                     tloc * D + (jj + 1) * 128],
                                            identity=eye[:])
                            t.drain()
                            t.sem_inc(s_pe.h, 1)
                            P.wait(t, f"E_fmevr_{tt}_{r}")
                    P.emit("tensor", t_trE)
                    for r in range(2):
                        P.event(f"E_trE_{tt}_{r}", s_pe)

                        def v_fmev(v, tt=tt, r=r, slot=slot):
                            P.wait(v, f"E_trE_{tt}_{r}")
                            v.tensor_copy(out=fmst[:, slot * 2048 + r * 1024:
                                                   slot * 2048 + (r + 1) * 1024],
                                          in_=pbt[:])
                            v.drain()
                            v.sem_inc(s_dv.h, 1)
                        P.emit("vector", v_fmev)
                        P.event(f"E_fmevr_{tt}_{r}", s_dv)
                    P.alias(f"E_fmev_{st}_{pr_i}_{ti}", f"E_fmevr_{tt}_1")
                    P.alias(f"E_trE_{tt}_3", f"E_trE_{tt}_1")

                obanks = ((pf[0], pf[1]), (pf[2], pf[3]))

                def t_gemm(t, st=st, pr_i=pr_i, tiles=tiles, obanks=obanks):
                    for ti in range(2):
                        P.wait(t, f"E_fmev_{st}_{pr_i}_{ti}")
                    P.wait(t, f"A_trx_{st * 4 + 3}")
                    if pr_i == 0:
                        P.wait(t, f"D_zl_{st * 4 + 3}")
                        P.wait(t, f"C_zw_{st}_3")
                    else:
                        P.wait(t, f"E_res_{st * 4 + 1}")
                    for fc in range(16):
                        P.wait(t, f"E_wol_{st}_{pr_i}_{fc}")
                        for ti in range(2):
                            lhs = fmst[:, ti * 2048 + fc * 128:ti * 2048 + (fc + 1) * 128]
                            for half in range(2):
                                t.matmul(obanks[ti][half][:, 0:512], lhs,
                                         wbf[:, (fc % 2) * D + half * 512:
                                             (fc % 2) * D + (half + 1) * 512],
                                         start=(fc == 0), stop=False)
                        if fc % 2 == 1:
                            t.drain()
                            t.sem_inc(s_pe.h, 1)
                    for dc in range(8):
                        P.wait(t, f"E_wql_{st}_{pr_i}_{dc}")
                        for ti, tloc in enumerate(tiles):
                            lhs = xqT[:, dc * 512 + tloc * 128:dc * 512 + (tloc + 1) * 128]
                            for half in range(2):
                                t.matmul(obanks[ti][half][:, 0:512], lhs,
                                         wbf2[:, (dc % 2) * D + half * 512:
                                              (dc % 2) * D + (half + 1) * 512],
                                         start=False, stop=False)
                        t.drain()
                        t.sem_inc(s_pe.h, 1)
                    P.wait(t, "dact0")
                    for ti in range(2):
                        for half in range(2):
                            t.matmul(obanks[ti][half][:, 0:512], ones1r[:],
                                     fbt[:, half * 512:(half + 1) * 512],
                                     start=False, stop=True)
                    t.drain()
                    t.sem_inc(s_pe.h, 1)
                P.emit("tensor", t_gemm)
                for fc2 in range(8):
                    P.event(f"E_womm_{st}_{pr_i}_{fc2}", s_pe)
                for dc in range(8):
                    P.event(f"E_ffmm_{st}_{pr_i}_{dc}", s_pe)
                P.event(f"E_bias_{st}_{pr_i}", s_pe)

                for fc in range(16):
                    dep = slot_user.get(("wbf", fc % 2), "prologue")

                    def s_wo(sy, fc=fc, dep=dep, st=st):
                        P.wait(sy, dep)
                        if st == 0:
                            P.wait(sy, "stg_done")
                        sy.dma_start(out=wbf[:, (fc % 2) * D:(fc % 2) * D + D],
                                     in_=wos_h[fc * 128:(fc + 1) * 128, :]
                                     ).then_inc(s_wb[fc % 2].h, 16)
                    P.emit("sync", s_wo)
                    P.event(f"E_wol_{st}_{pr_i}_{fc}", s_wb[fc % 2], 16)
                    slot_user[("wbf", fc % 2)] = f"E_womm_{st}_{pr_i}_{fc // 2}"
                for dc in range(8):
                    dep = slot_user.get(("wbf2", dc % 2), "prologue")

                    def s_wq(sy, dc=dc, dep=dep, st=st):
                        P.wait(sy, dep)
                        if st == 0:
                            P.wait(sy, "prologue")
                        sy.dma_start(out=wbf2[:, (dc % 2) * D:(dc % 2) * D + D],
                                     in_=wqt_h[dc * 128:(dc + 1) * 128, :]
                                     ).then_inc(s_v2[dc % 2].h, 16)
                    P.emit("sync", s_wq)
                    P.event(f"E_wql_{st}_{pr_i}_{dc}", s_v2[dc % 2], 16)
                    slot_user[("wbf2", dc % 2)] = f"E_ffmm_{st}_{pr_i}_{dc}"

                for ti, tloc in enumerate(tiles):
                    tt = st * 4 + tloc

                    def s_xr_f(sy, tt=tt, st=st, pr_i=pr_i, ti=ti):
                        pr = f"E_res_{tt - 1}"
                        if pr in P.ev:
                            P.wait(sy, pr)
                        sy.dma_start(out=xres[:], in_=x_h[tt * 128:(tt + 1) * 128, :]
                                     ).then_inc(s_xr.h, 16)
                    P.emit("sync", s_xr_f)
                    P.event(f"E_xr_{tt}", s_xr, 16)

                    def v_res(v, tt=tt, st=st, pr_i=pr_i, ti=ti, obanks=obanks):
                        P.wait(v, f"E_xr_{tt}")
                        if tt == 0:
                            P.wait(v, "oout_free")
                        if f"E_store_{tt - 1}" in P.ev:
                            P.wait(v, f"E_store_{tt - 1}")
                        v.tensor_scalar(out=oout[:], in0=xres[:],
                                        scalar1=afin[:, tt:tt + 1],
                                        scalar2=bfin[:, tt:tt + 1],
                                        op0=ALU.mult, op1=ALU.add)
                        v.drain()
                        P.wait(v, f"E_bias_{st}_{pr_i}")
                        for half in range(2):
                            v.tensor_tensor(out=oout[:, half * 512:(half + 1) * 512],
                                            in0=oout[:, half * 512:(half + 1) * 512],
                                            in1=obanks[ti][half][:, 0:512],
                                            op=ALU.add)
                        v.drain()
                        v.sem_inc(s_dv.h, 1)
                    P.emit("vector", v_res)
                    P.event(f"E_res_{tt}", s_dv)

                    def a_store(a, tt=tt):
                        P.wait(a, f"E_res_{tt}")
                        a.dma_start(out=out_h[tt * 128:(tt + 1) * 128, :],
                                    in_=oout[:]).then_inc(s_so[tt % 2].h, 16)
                    P.emit("scalar", a_store)
                    P.event(f"E_store_{tt}", s_so[tt % 2], 16)
            P.alias(f"E_done_{st}", f"E_store_{st * 4 + 3}")

        import os as _os
        _lim = _os.environ.get("KMK_PHASES", "e")
        for st in range(NST):
            if _lim == "p":
                break
            if st == 0:
                emit_staging()
            if _lim == "a":
                continue
            phase_B(st)
            if _lim == "b":
                continue
            if st == 0 and _lim not in ("c",):
                emit_fwq_q(0)
                emit_fwq_q(1)
            for lc in range(4):
                if st == 0 and lc >= 1 and _lim not in ("c",):
                    emit_fwq_t(2 * (lc - 1), lc - 1)
                    emit_fwq_t(2 * lc - 1, lc - 1)
                phase_C(st, lc)
                if _lim != "c":
                    phase_D(st, lc)
                if st == 0 and lc <= 2 and _lim not in ("c",):
                    emit_fwq_q(2 * lc + 2)
                    emit_fwq_q(2 * lc + 3)
                if st >= 1 and st + 1 < NST and lc <= 2 and _lim == "e":
                    phase_A((st + 1) * 4 + lc, pbt_pred=st * 4 + lc)
            if st == 0 and _lim not in ("c", "d"):
                drip(99)
                emit_fwq_t(6, 3)
                emit_fwq_t(7, 3)
            if _lim in ("c", "d"):
                continue
            emit_halo(st)
            if st + 1 < NST:
                if st == 0:
                    phase_A((st + 1) * 4 + 0, pbt_pred=3)
                    for lc in range(1, 4):
                        phase_A((st + 1) * 4 + lc)
                else:
                    phase_A((st + 1) * 4 + 3, pbt_pred=st * 4 + 3)
            phase_E(st)

        def s_final(sy):
            for s in (s_so[0], s_so[1], s_vd, s_dv, s_ac, s_pe, s_gp, s_ld,
                      s_cb, s_xr, s_x2[0], s_x2[1], s_wb[0], s_wb[1],
                      s_v2[0], s_v2[1], s_sg[0], s_sg[1],
                      s_sv[0], s_sv[1], s_sv[2], s_sv[3]):
                if s.n:
                    sy.wait_ge(s.h, s.n)
        P.emit("sync", s_final)

        with nc.Block() as block:
            @block.sync
            def _(eng):
                for fn in P.ops["sync"]:
                    fn(eng)

            @block.gpsimd
            def _(eng):
                for fn in P.ops["gpsimd"]:
                    fn(eng)

            @block.vector
            def _(eng):
                for fn in P.ops["vector"]:
                    fn(eng)

            @block.scalar
            def _(eng):
                for fn in P.ops["scalar"]:
                    fn(eng)

            @block.tensor
            def _(eng):
                geng = PEGuard(eng, eye[:, 0:1])
                for fn in P.ops["tensor"]:
                    fn(geng)

    return nc


_CACHE = {}


def _get_program(T):
    if T not in _CACHE:
        _CACHE[T] = build_program(T)
    return _CACHE[T]


def kernel(L0_attn_x, L0_attn_norm_w, L0_attn_Wq, L0_attn_Wk, L0_attn_Wv,
           L0_attn_Wqf, L0_attn_Wkf, L0_attn_Wout, L0_ffn_weight, L0_ffn_bias,
           L0_ffn_delta_w, L0_ffn_delta_act):
    x = np.ascontiguousarray(np.asarray(L0_attn_x, dtype=np.float32))
    B, T, _ = x.shape
    nc = _get_program(T)
    shared = {
        "norm_w": np.ascontiguousarray(
            np.asarray(L0_attn_norm_w, dtype=np.float32).reshape(8, 128)),
        "Wq": np.ascontiguousarray(np.asarray(L0_attn_Wq, dtype=np.float32)),
        "Wk": np.ascontiguousarray(np.asarray(L0_attn_Wk, dtype=np.float32)),
        "Wv": np.ascontiguousarray(np.asarray(L0_attn_Wv, dtype=np.float32)),
        "Wqf": np.ascontiguousarray(np.asarray(L0_attn_Wqf, dtype=np.float32)),
        "Wkf": np.ascontiguousarray(np.asarray(L0_attn_Wkf, dtype=np.float32)),
        "Wout": np.ascontiguousarray(np.asarray(L0_attn_Wout, dtype=np.float32)),
        "fw": np.ascontiguousarray(np.asarray(L0_ffn_weight, dtype=np.float32)),
        "fb": np.ascontiguousarray(
            np.asarray(L0_ffn_bias, dtype=np.float32).reshape(1, D)),
        "dw": np.ascontiguousarray(np.asarray(L0_ffn_delta_w, dtype=np.float32)),
        "da": np.ascontiguousarray(
            np.asarray(L0_ffn_delta_act, dtype=np.float32).reshape(1, 1)),
    }
    in_maps = [dict(shared, x=np.ascontiguousarray(x[i])) for i in range(B)]
    res = run_bass_kernel_spmd(nc, in_maps, core_ids=list(range(B)))
    out = np.stack([np.asarray(r["out"], dtype=np.float32) for r in res.results], 0)
    return out


# version 22
# speedup vs baseline: 1.0432x; 1.0432x over previous
import contextlib
import numpy as np
import concourse.bass as bass
import concourse.mybir as mybir
from concourse.bass_utils import run_bass_kernel_spmd

F32 = mybir.dt.float32
BF16 = mybir.dt.bfloat16
AF = mybir.ActivationFunctionType
ALU = mybir.AluOpType
AX = mybir.AxisListType
LN2 = 0.6931471805599453

D = 1024
H = 16
DH = 64
FT = 16
NBLK = 32
VW = H * 65

ENGS = ("sync", "gpsimd", "vector", "scalar", "tensor")


class Sem:
    def __init__(self, handle, name):
        self.h = handle
        self.name = name
        self.n = 0


class Plan:
    def __init__(self):
        self.ops = {e: [] for e in ENGS}
        self.ev = {}

    def emit(self, eng, fn):
        self.ops[eng].append(fn)

    def event(self, name, sem, k=1):
        sem.n += k
        self.ev[name] = (sem, sem.n)

    def alias(self, name, other):
        self.ev[name] = self.ev[other]

    def wait(self, api, name):
        sem, v = self.ev[name]
        api.wait_ge(sem.h, v)


class PEGuard:

    def __init__(self, eng, sep_op):
        self._e = eng
        self._sep = sep_op
        self._cfg = None

    def __getattr__(self, name):
        return getattr(self._e, name)

    def transpose(self, *a, **k):
        self._cfg = None
        return self._e.transpose(*a, **k)

    def matmul(self, out, lhsT, rhs, **k):
        if k.get("is_transpose"):
            self._cfg = None
            return self._e.matmul(out, lhsT, rhs, **k)
        cfg = (lhsT.partition_size(), lhsT.base_partition())
        if (self._cfg is not None and cfg[0] == self._cfg[0]
                and cfg[1] != self._cfg[1]):
            assert k.get("start") is True, "PE separator needs start=True site"
            self._e.matmul(out[0:1, 0:1], self._sep, self._sep,
                           start=True, stop=True)
        self._cfg = cfg
        return self._e.matmul(out, lhsT, rhs, **k)


def build_program(T=2048):
    NT = T // 128
    NST = NT // 4
    nc = bass.Bass("TRN2", target_bir_lowering=False, debug=False)

    x_h = nc.dram_tensor("x", [T, D], F32, kind="ExternalInput").ap()
    nw_h = nc.dram_tensor("norm_w", [8, 128], F32, kind="ExternalInput").ap()
    wq_h = nc.dram_tensor("Wq", [D, D], F32, kind="ExternalInput").ap()
    wk_h = nc.dram_tensor("Wk", [D, D], F32, kind="ExternalInput").ap()
    wv_h = nc.dram_tensor("Wv", [D, D], F32, kind="ExternalInput").ap()
    wqf_h = nc.dram_tensor("Wqf", [DH, FT], F32, kind="ExternalInput").ap()
    wkf_h = nc.dram_tensor("Wkf", [DH, FT], F32, kind="ExternalInput").ap()
    wo_h = nc.dram_tensor("Wout", [2 * D, D], F32, kind="ExternalInput").ap()
    fw_h = nc.dram_tensor("fw", [D, D], F32, kind="ExternalInput").ap()
    fb_h = nc.dram_tensor("fb", [1, D], F32, kind="ExternalInput").ap()
    dw_h = nc.dram_tensor("dw", [D, NBLK], F32, kind="ExternalInput").ap()
    da_h = nc.dram_tensor("da", [1, 1], F32, kind="ExternalInput").ap()
    out_h = nc.dram_tensor("out", [T, D], F32, kind="ExternalOutput").ap()

    wqs_h = nc.dram_tensor("wq_s", [D, D], BF16, kind="Internal").ap()
    wks_h = nc.dram_tensor("wk_s", [D, D], BF16, kind="Internal").ap()
    wvs_h = nc.dram_tensor("wv_s", [D, D], BF16, kind="Internal").ap()
    wos_h = nc.dram_tensor("wo_s", [2 * D, D], BF16, kind="Internal").ap()
    wqt_h = nc.dram_tensor("wqt_s", [D, D], BF16, kind="Internal").ap()
    sc1_h = nc.dram_tensor("sc1_s", [1, 1], F32, kind="Internal").ap()
    sc2_h = nc.dram_tensor("sc2_s", [1, 1], F32, kind="Internal").ap()

    es = contextlib.ExitStack()
    with es:
        def sb(name, shape, dt=BF16):
            return es.enter_context(nc.sbuf_tensor(name, shape, dt))

        def psum(name, shape, dt=F32):
            return es.enter_context(nc.psum_tensor(name, shape, dt))

        def mksem(name):
            return Sem(es.enter_context(nc.semaphore(name)), name)

        iof = sb("iof", [128, 128], F32)
        iop = sb("iop", [128, 1], F32)
        eye = sb("eye", [128, 128], BF16)
        eyef = sb("eyef", [128, 128], F32)
        trilq = sb("trilq", [128, 128], BF16)
        maskA = sb("maskA", [128, 128], BF16)
        maskB = sb("maskB", [128, 128], BF16)
        cbias = sb("cbias", [128, 1], F32)
        bias_b = sb("bias_b", [128, 8], F32)
        dat = sb("dat", [128, 1], F32)
        fbt = sb("fbt", [1, D], BF16)
        nwt = sb("nwt", [128, 8], F32)
        ones1r = sb("ones1r", [1, 128], BF16)
        g2t = sb("g2t", [128, 2], BF16)
        st_s1 = sb("st_s1", [128, 1], F32)
        st_s2 = sb("st_s2", [128, 1], F32)
        st_m = sb("st_m", [128, 1], F32)
        st_v = sb("st_v", [128, 1], F32)
        st_t = sb("st_t", [128, 2], F32)
        st_rs = sb("st_rs", [128, 1], F32)
        st_c2 = sb("st_c2", [128, 1], F32)
        afin = sb("afin", [128, NT], F32)
        bfin = sb("bfin", [128, NT], F32)
        mxs = sb("mxs", [2, 20], F32)
        xt = sb("xt", [128, 2 * D], F32)
        xnb = sb("xnb", [128, D], F32)
        hb = sb("hb", [128, D], BF16)
        xqb = sb("xqb", [128, 3 * D], BF16)

        q_am = sb("q_am", [128, 2 * NBLK], F32)
        q_e = sb("q_e", [128, 2 * NBLK], F32)
        q_d = sb("q_d", [128, NBLK], F32)
        q_r2 = sb("q_r2", [128, 2 * NBLK], F32)
        q_sh = sb("q_sh", [128, 2 * NBLK], F32)
        q_ab = sb("q_ab", [128, 2 * D], F32)
        q_sg = sb("q_sg", [128, 2 * D], BF16)
        q_c = sb("q_c", [128, 512], F32)
        q_u = [sb(f"q_u{i}", [128, 512], BF16) for i in range(4)]

        hT = sb("hT", [128, 8 * 512], BF16)
        xqT = sb("xqT", [128, 8 * 512], BF16)
        qT = sb("qT", [128, 8 * 512], BF16)
        kT = sb("kT", [128, 8 * 512], BF16)
        kThal = sb("kThal", [128, 8 * 128], BF16)
        v65 = sb("v65", [128, 4 * VW], BF16)
        v65hal = sb("v65hal", [128, VW], BF16)
        qftm = sb("qftm", [128, 4 * 512], BF16)
        kftm = sb("kftm", [128, 4 * 512], BF16)
        qfFM = sb("qfFM", [64, 1024], BF16)
        kfFM = sb("kfFM", [64, 1024], BF16)

        psiq = sb("psiq", [128, H * 128], BF16)
        psik = sb("psik", [128, H * 128], BF16)
        phiq = sb("phiq", [128, H * 128], BF16)
        attA = sb("attA", [128, H * 128], BF16)
        attB = sb("attB", [128, H * 128], BF16)
        atil = sb("atil", [128, H * 128], BF16)
        kv_ab = sb("kv_ab", [128, H * 65], BF16)
        kva_sb = sb("kva_sb", [128, H * 65], F32)
        kvb_sb = sb("kvb_sb", [64, 8 * 65], F32)
        kv_bb = sb("kv_bb", [64, 8 * 65], BF16)
        lin_tm = sb("lin_tm", [128, 4 * D], BF16)
        win_tm = sb("win_tm", [128, 4 * D], BF16)
        zs = sb("zs", [128, H], F32)
        rz = sb("rz", [128, H], F32)
        sqs = sb("sqs", [128, 512], BF16)
        sqs2 = sb("sqs2", [128, 512], BF16)
        fmst = sb("fmst", [128, 2 * 2048], BF16)
        wst = sb("wst", [128, D], F32)
        wbf = sb("wbf", [128, 2 * D], BF16)
        wbf2 = sb("wbf2", [128, 2 * D], BF16)
        wqfbd = sb("wqfbd", [128, 64], BF16)
        xres = sb("xres", [128, D], F32)
        oout = sb("oout", [128, D], F32)

        pbt = psum("pbt", [128, 1024], BF16)
        pf = [psum(f"pf{i}", [128, 512], F32) for i in range(4)]
        kva_p = psum("kva_p", [128, 455], F32)
        kva_p2 = psum("kva_p2", [128, 455], F32)
        kvmix = psum("kvmix", [128, 512], F32)

        s_ld = mksem("s_ld")
        s_vd = mksem("s_vd")
        s_x2 = [mksem("s_xa"), mksem("s_xb")]
        s_sg = [mksem("s_sga"), mksem("s_sgb")]
        s_sv = [mksem(f"s_sv{i}") for i in range(4)]
        s_wb = [mksem("s_wba"), mksem("s_wbb")]
        s_v2 = [mksem("s_v2a"), mksem("s_v2b")]
        s_so = [mksem("s_soa"), mksem("s_sob")]
        s_cb = mksem("s_cb")
        s_xr = mksem("s_xr")
        s_dv = mksem("s_dv")
        s_ac = mksem("s_ac")
        s_gp = mksem("s_gp")
        s_pe = mksem("s_pe")
        s_st = mksem("s_st")

        P = Plan()
        slot_user = {}
        B_EPSN, B_C2, B_XE, B_XS, B_WE, B_WS, B_ONE, B_SPARE = range(8)

        fillers = []

        def drip(n=1):
            k = 0
            while fillers and k < n:
                fn, evn = fillers.pop(0)
                P.emit("vector", fn)
                P.event(evn, s_dv)
                k += 1

        qtile_hist = []
        xq_par = {}

        def quant_emit(tag, src_f32, e_lo, k2, wait_evs, delta_tile,
                       reader_ev):
            par = len(qtile_hist) % 2
            parx = len(qtile_hist) % 3
            xq_par[tag] = parx
            war = [qtile_hist[-3][1]] if len(qtile_hist) >= 3 else []
            prev_tag = qtile_hist[-2][0] if len(qtile_hist) >= 2 else None
            qtile_hist.append((tag, reader_ev))
            qab = q_ab[:, par * D:(par + 1) * D]
            qsg = q_sg[:, par * D:(par + 1) * D]
            qam = q_am[:, par * NBLK:(par + 1) * NBLK]
            qe = q_e[:, par * NBLK:(par + 1) * NBLK]
            qr2 = q_r2[:, par * NBLK:(par + 1) * NBLK]
            qsh = q_sh[:, par * NBLK:(par + 1) * NBLK]
            out_bf = xqb[:, parx * D:(parx + 1) * D]

            def a_q(a, wait_evs=tuple(wait_evs), prev_tag=prev_tag):
                for evn in wait_evs:
                    P.wait(a, evn)
                if prev_tag is not None:
                    P.wait(a, prev_tag)
                a.activation(out=qab, in_=src_f32, func=AF.Abs)
                a.activation(out=qsg, in_=src_f32, func=AF.Sign)
                if delta_tile is not None:
                    a.activation(out=q_d[:], in_=delta_tile, func=AF.Tanh,
                                 scale=4.0)
                a.drain()
                a.sem_inc(s_ac.h, 1)
            P.emit("scalar", a_q)
            P.event(tag + "_abs", s_ac)

            def v_qe(v):
                P.wait(v, tag + "_abs")
                v.tensor_reduce(out=qam,
                                in_=qab.rearrange("p (g k) -> p g k", k=32),
                                axis=AX.X, op=ALU.max)
                v.drain()
                v.tensor_scalar(out=qam, in0=qam, scalar1=2.0 ** -50,
                                scalar2=None, op0=ALU.max)
                v.drain()
                v.tensor_scalar(out=qe, in0=qam, scalar1=2.0 ** (e_lo + 1),
                                scalar2=None, op0=ALU.is_ge)
                for j in range(e_lo + 2, e_lo + 15):
                    v.drain()
                    v.scalar_tensor_tensor(out=qe, in0=qam, scalar=2.0 ** j,
                                           in1=qe, op0=ALU.is_ge, op1=ALU.add)
                v.drain()
                if delta_tile is not None:
                    v.scalar_tensor_tensor(out=qe, in0=q_d[:], scalar=0.5,
                                           in1=qe, op0=ALU.mult, op1=ALU.add)
                else:
                    v.tensor_scalar(out=qe, in0=qe, scalar1=dat[:, 0:1],
                                    scalar2=None, op0=ALU.add)
                v.drain()
                v.sem_inc(s_dv.h, 1)
            P.emit("vector", v_qe)
            P.event(tag + "_e", s_dv)

            def a_q2(a):
                P.wait(a, tag + "_e")
                a.activation(out=qr2, in_=qe, func=AF.Exp, scale=-LN2)
                a.activation(out=qsh, in_=qe, func=AF.Exp, scale=LN2)
                a.drain()
                a.sem_inc(s_ac.h, 1)
            P.emit("scalar", a_q2)
            P.event(tag + "_sc", s_ac)

            for hi in range(2):
                def v_qh(v, hi=hi, war=tuple(war)):
                    P.wait(v, tag + "_sc")
                    if hi == 0:
                        for evn in war:
                            if evn in P.ev:
                                P.wait(v, evn)
                    cl = hi * 512
                    hab = qab[:, cl:cl + 512].rearrange("p (g k) -> p g k",
                                                        k=32)
                    hsg = qsg[:, cl:cl + 512]
                    hr2 = qr2[:, hi * 16:(hi + 1) * 16]
                    hsh = qsh[:, hi * 16:(hi + 1) * 16]
                    v.scalar_tensor_tensor(
                        out=q_c[:].rearrange("p (g k) -> p g k", k=32),
                        in0=hab, scalar=float(k2),
                        in1=hr2[:, :, None].broadcast_to([128, 16, 32]),
                        op0=ALU.mult, op1=ALU.mult)
                    v.drain()
                    v.tensor_scalar(out=q_u[0][:], in0=q_c[:], scalar1=0.5,
                                    scalar2=None, op0=ALU.is_ge)
                    v.tensor_scalar(out=q_u[1][:], in0=q_c[:], scalar1=1.5,
                                    scalar2=None, op0=ALU.is_ge)
                    v.tensor_scalar(out=q_u[2][:], in0=q_c[:], scalar1=2.5,
                                    scalar2=None, op0=ALU.is_ge)
                    v.tensor_scalar(out=q_u[3][:], in0=q_c[:], scalar1=3.5,
                                    scalar2=None, op0=ALU.is_ge)
                    v.drain()
                    v.tensor_add(out=q_u[0][:], in0=q_u[0][:], in1=q_u[1][:])
                    v.tensor_add(out=q_u[2][:], in0=q_u[2][:], in1=q_u[3][:])
                    v.drain()
                    v.tensor_scalar(out=q_u[1][:], in0=q_c[:], scalar1=5.0,
                                    scalar2=2.0, op0=ALU.is_ge, op1=ALU.mult)
                    v.tensor_scalar(out=q_u[3][:], in0=q_c[:], scalar1=7.0,
                                    scalar2=2.0, op0=ALU.is_ge, op1=ALU.mult)
                    v.drain()
                    v.tensor_add(out=q_u[0][:], in0=q_u[0][:], in1=q_u[2][:])
                    v.tensor_add(out=q_u[1][:], in0=q_u[1][:], in1=q_u[3][:])
                    v.drain()
                    v.tensor_scalar(out=q_u[2][:], in0=q_c[:], scalar1=10.0,
                                    scalar2=4.0, op0=ALU.is_ge, op1=ALU.mult)
                    v.drain()
                    v.tensor_add(out=q_u[0][:], in0=q_u[0][:], in1=q_u[1][:])
                    v.drain()
                    v.tensor_add(out=q_u[0][:], in0=q_u[0][:], in1=q_u[2][:])
                    v.drain()
                    v.tensor_mul(out=q_u[0][:], in0=q_u[0][:], in1=hsg)
                    v.drain()
                    v.scalar_tensor_tensor(
                        out=out_bf[:, cl:cl + 512]
                        .rearrange("p (g k) -> p g k", k=32),
                        in0=q_u[0][:].rearrange("p (g k) -> p g k", k=32),
                        scalar=float(1.0 / k2),
                        in1=hsh[:, :, None].broadcast_to([128, 16, 32]),
                        op0=ALU.mult, op1=ALU.mult)
                    v.drain()
                    v.sem_inc(s_dv.h, 1)
                fillers.append((v_qh, tag + "_h0" if hi == 0 else tag))

        def g_iota(g):
            g.iota(iof[:], [[1, 128]], channel_multiplier=0,
                   allow_small_or_imprecise_dtypes=True)
            g.iota(iop[:], [[0, 1]], channel_multiplier=1,
                   allow_small_or_imprecise_dtypes=True).then_inc(s_gp.h, 1)
        P.emit("gpsimd", g_iota)
        P.event("iota", s_gp)

        def s_small(sy):
            sy.dma_start(out=nwt[:], in_=nw_h.transpose([1, 0]), allow_slow_non_contiguous=True).then_inc(s_ld.h, 16)
            sy.dma_start(out=dat[0:1, 0:1], in_=da_h[:]).then_inc(s_ld.h, 16)
            sy.dma_start(out=xnb[0:1, 0:D], in_=fb_h[:]).then_inc(s_ld.h, 16)
            sy.dma_start(out=wst[0:DH, 0:FT], in_=wqf_h[:]).then_inc(s_ld.h, 16)
            sy.dma_start(out=wst[0:DH, FT:2 * FT], in_=wkf_h[:]).then_inc(s_ld.h, 16)
        P.emit("sync", s_small)
        P.event("small_ld", s_ld, 16 * 5)

        def v_const(v):
            P.wait(v, "iota")
            for col, val in ((B_EPSN, 1e-5), (B_C2, 1e-11),
                             (B_XE, LN2 * 12.0), (B_XS, -LN2 * 12.0),
                             (B_WE, LN2 * 14.0), (B_WS, -LN2 * 14.0),
                             (B_ONE, 1.0), (B_SPARE, 0.0)):
                v.memset(bias_b[:, col:col + 1], val)
            v.memset(ones1r[:], 1.0)
            v.memset(mxs[:], 0.0)
            v.memset(g2t[:], 0.0)
            v.drain()
            v.memset(g2t[0:64, 0:1], 1.0)
            v.memset(g2t[64:128, 1:2], 1.0)
            v.tensor_scalar(out=eye[:], in0=iof[:], scalar1=iop[:], scalar2=None,
                            op0=ALU.is_equal)
            v.tensor_scalar(out=eyef[:], in0=iof[:], scalar1=iop[:], scalar2=None,
                            op0=ALU.is_equal)
            v.tensor_scalar(out=trilq[:], in0=iof[:], scalar1=iop[:], scalar2=None,
                            op0=ALU.is_ge)
            v.tensor_scalar(out=st_s1[:], in0=iop[:], scalar1=64.0, scalar2=None,
                            op0=ALU.subtract)
            v.drain()
            v.tensor_scalar(out=maskA[:], in0=iof[:], scalar1=st_s1[:],
                            scalar2=None, op0=ALU.is_le)
            v.tensor_scalar(out=maskB[:], in0=iof[:], scalar1=iop[:], scalar2=None,
                            op0=ALU.is_ge)
            v.tensor_scalar(out=st_t[:, 0:1], in0=iop[:], scalar1=64.0, scalar2=None,
                            op0=ALU.add)
            v.drain()
            v.tensor_scalar(out=q_c[:, 0:128], in0=iof[:], scalar1=st_t[:, 0:1],
                            scalar2=None, op0=ALU.is_le)
            v.drain()
            v.tensor_mul(out=maskB[:], in0=maskB[:], in1=q_c[:, 0:128])
            for buf in (psiq, psik):
                bv = buf[:].rearrange("p (h c) -> p h c", c=128)
                v.memset(bv[:, :, 120:121], 1.0)
                v.memset(bv[:, :, 121:128], 0.0)
            v.drain()
            v.sem_inc(s_dv.h, 1)
        P.emit("vector", v_const)
        P.event("consts", s_dv)

        def a_dact(a):
            P.wait(a, "small_ld")
            a.activation(out=dat[0:1, 0:1], in_=dat[0:1, 0:1], func=AF.Tanh, scale=4.0)
            a.drain()
            a.mul(out=dat[0:1, 0:1], in_=dat[0:1, 0:1], mul=0.5)
            a.drain()
            a.copy(out=fbt[:], in_=xnb[0:1, 0:D])
            a.drain()
            a.sem_inc(s_ac.h, 1)
        P.emit("scalar", a_dact)
        P.event("dact0", s_ac)

        def t_bcast(t):
            P.wait(t, "dact0")
            P.wait(t, "consts")
            t.matmul(pf[3][:, 0:1], ones1r[:], dat[0:1, 0:1].bitcast(BF16)[0:1, 0:2],
                     start=True, stop=True).then_inc(s_pe.h, 1)
        def s_dbc(sy):
            P.wait(sy, "dact0")
            sy.dma_start(out=sc1_h, in_=dat[0:1, 0:1]).then_inc(s_cb.h, 16)
        P.emit("sync", s_dbc)
        P.event("dat_d", s_cb, 16)

        def s_dbc2(sy):
            P.wait(sy, "dat_d")
            sy.dma_start(out=dat[:], in_=sc1_h.broadcast_to([128, 1])
                         ).then_inc(s_cb.h, 16)
        P.emit("sync", s_dbc2)
        P.event("dat_bc", s_cb, 16)

        def v_wqfb(v):
            P.wait(v, "small_ld")
            P.wait(v, "consts")
            v.memset(wqfbd[:], 0.0)
            v.drain()
            v.tensor_copy(out=wqfbd[0:DH, 0:FT], in_=wst[0:DH, 0:FT])
            v.tensor_copy(out=wqfbd[DH:128, FT:32], in_=wst[0:DH, 0:FT])
            v.tensor_copy(out=wqfbd[0:DH, 32:48], in_=wst[0:DH, FT:2 * FT])
            v.tensor_copy(out=wqfbd[DH:128, 48:64], in_=wst[0:DH, FT:2 * FT])
            v.drain()
            v.sem_inc(s_dv.h, 1)
        P.emit("vector", v_wqfb)
        P.event("pro0", s_dv)

        def emit_staging():
            wplans = [(wq_h, wqs_h, True, 8), (wk_h, wks_h, True, 8),
                      (wv_h, wvs_h, True, 8), (wo_h, wos_h, False, 16)]
            wtiles = [(src, dst, fold, ci)
                      for (src, dst, fold, nch) in wplans for ci in range(nch)]
            wfold_slots = [wbf2[:, 0:D], wbf2[:, D:2 * D]]
            for i, (src, dst, fold, ci) in enumerate(wtiles):
                if i in (4, 10, 16, 22):
                    phase_A({4: 0, 10: 1, 16: 2, 22: 3}[i])
                lbuf = wst if i % 2 == 0 else oout
                fslot = wfold_slots[i % 2]

                def s_w(sy, src=src, ci=ci, i=i, lbuf=lbuf):
                    if i == 0:
                        P.wait(sy, "pro0")
                    if i >= 2:
                        P.wait(sy, f"wcast_{i - 2}")
                    sy.dma_start(out=lbuf[:], in_=src[ci * 128:(ci + 1) * 128, :]
                                 ).then_inc(s_sg[i % 2].h, 16)
                P.emit("sync", s_w)
                P.event(f"wld_{i}", s_sg[i % 2], 16)

                def v_w(v, fold=fold, ci=ci, i=i, lbuf=lbuf, fslot=fslot):
                    P.wait(v, f"wld_{i}")
                    if i >= 2:
                        P.wait(v, f"wscr_{i - 2}")
                    if fold:
                        v.tensor_scalar(out=fslot, in0=lbuf[:],
                                        scalar1=nwt[:, (ci % 8):(ci % 8) + 1],
                                        scalar2=None, op0=ALU.mult)
                    else:
                        v.tensor_copy(out=fslot, in_=lbuf[:])
                    v.drain()
                    v.sem_inc(s_dv.h, 1)
                P.emit("vector", v_w)
                P.event(f"wcast_{i}", s_dv)

                def a_wd(a, ci=ci, i=i, dst=dst, fslot=fslot):
                    P.wait(a, f"wcast_{i}")
                    a.dma_start(out=dst[ci * 128:(ci + 1) * 128, :],
                                in_=fslot).then_inc(s_sv[i % 4].h, 16)
                P.emit("scalar", a_wd)
                P.event(f"wscr_{i}", s_sv[i % 4], 16)
            P.alias("stg_done", "wscr_39")
            P.alias("oout_free", "wcast_39")
            slot_user[("wbf", 0)] = "pro0"
            slot_user[("wbf", 1)] = "pro0"
            slot_user[("wbf2", 0)] = "wscr_38"
            slot_user[("wbf2", 1)] = "wscr_39"

        def emit_fwq_q(ci):
            def s_fw(sy, ci=ci):
                P.wait(sy, f"fwq_{ci - 1}_e" if ci >= 1 else "stg_done")
                sy.dma_start(out=wst[:], in_=fw_h[ci * 128:(ci + 1) * 128, :]
                             ).then_inc(s_ld.h, 16)
                sy.dma_start(out=q_d[:], in_=dw_h[ci * 128:(ci + 1) * 128, :]
                             ).then_inc(s_ld.h, 16)
            P.emit("sync", s_fw)
            P.event(f"fwld_{ci}", s_ld, 32)
            quant_emit(f"fwq_{ci}", wst[:], -11, 2.0 ** 14,
                       [f"fwld_{ci}", "dat_bc"], q_d[:],
                       reader_ev=f"fwtr_{ci}")

        def emit_fwq_t(ci, chunk):
            par = xq_par[f"fwq_{ci}"]

            def t_fw(t, ci=ci, chunk=chunk, par=par):
                P.wait(t, f"fwq_{ci}")
                if ci >= 1:
                    P.wait(t, f"wqtc_{ci - 1}")
                if f"D_phiev_{chunk}_3" in P.ev:
                    P.wait(t, f"D_phiev_{chunk}_3")
                for j in range(8):
                    t.transpose(out=pbt[:, j * 128:(j + 1) * 128],
                                in_=xqb[:, par * D + j * 128:
                                        par * D + (j + 1) * 128],
                                identity=eye[:])
                t.drain()
                t.sem_inc(s_pe.h, 1)
            P.emit("tensor", t_fw)
            P.event(f"fwtr_{ci}", s_pe)

            def v_fw(v, ci=ci):
                P.wait(v, f"fwtr_{ci}")
                P.wait(v, "B_vmm_0_1_7")
                if ci >= 1:
                    P.wait(v, f"wqt_{ci - 1}")
                v.tensor_copy(out=hT[:, 0:1024], in_=pbt[:])
                v.drain()
                v.sem_inc(s_dv.h, 1)
            P.emit("vector", v_fw)
            P.event(f"wqtc_{ci}", s_dv)

            def a_fwd(a, ci=ci):
                P.wait(a, f"wqtc_{ci}")
                a.dma_start(
                    out=wqt_h[:, ci * 128:(ci + 1) * 128]
                    .rearrange("(j p) c -> p j c", p=128),
                    in_=hT[:, 0:1024].rearrange("p (j c) -> p j c", c=128)
                    ).then_inc(s_vd.h, 16)
            P.emit("scalar", a_fwd)
            P.event(f"wqt_{ci}", s_vd, 16)
            if ci == 7:
                P.alias("prologue", "wqt_7")

        def phase_A(tt, pbt_pred=None):
            par = tt % 2
            xtile = xt[:, par * D:(par + 1) * D]
            lc = tt % 4
            st = tt // 4
            if lc >= 2:
                drip(2)

            def s_x(sy, tt=tt, xtile=xtile):
                if tt >= 2:
                    P.wait(sy, f"A_xtfree_{tt - 2}")
                sy.dma_start(out=xtile, in_=x_h[tt * 128:(tt + 1) * 128, :]
                             ).then_inc(s_x2[tt % 2].h, 16)
            P.emit("sync", s_x)
            P.event(f"A_x_{tt}", s_x2[tt % 2], 16)

            def a_st(a, tt=tt, xtile=xtile):
                P.wait(a, f"A_x_{tt}")
                if tt >= 1:
                    P.wait(a, f"A_xq_{tt - 1}_abs")
                    P.wait(a, f"A_var_{tt - 1}")
                a.activation(out=xnb[:], in_=xtile, func=AF.Square,
                             accum_out=st_s2[:])
                a.drain()
                a.sem_inc(s_ac.h, 1)
            P.emit("scalar", a_st)
            P.event(f"A_sq_{tt}", s_ac)

            def v_st(v, tt=tt, xtile=xtile):
                P.wait(v, f"A_x_{tt}")
                v.tensor_reduce(out=st_s1[:], in_=xtile, axis=AX.X, op=ALU.add)
                v.drain()
                v.tensor_scalar(out=st_m[:], in0=st_s1[:], scalar1=1.0 / D,
                                scalar2=None, op0=ALU.mult)
                P.wait(v, f"A_sq_{tt}")
                v.tensor_scalar(out=st_v[:], in0=st_s2[:], scalar1=1.0 / D,
                                scalar2=None, op0=ALU.mult)
                v.drain()
                v.tensor_mul(out=st_t[:, 0:1], in0=st_m[:], in1=st_m[:])
                v.drain()
                v.tensor_sub(out=st_v[:], in0=st_v[:], in1=st_t[:, 0:1])
                v.drain()
                v.sem_inc(s_dv.h, 1)
            P.emit("vector", v_st)
            P.event(f"A_var_{tt}", s_dv)

            def a_st2(a, tt=tt):
                P.wait(a, f"A_var_{tt}")
                a.activation(out=st_s1[:], in_=st_v[:], func=AF.Sqrt,
                             bias=bias_b[:, B_EPSN:B_EPSN + 1])
                a.activation(out=st_s2[:], in_=st_v[:], func=AF.Sqrt,
                             scale=1.0 + 1e-6, bias=bias_b[:, B_C2:B_C2 + 1])
                a.drain()
                a.sem_inc(s_ac.h, 1)
            P.emit("scalar", a_st2)
            P.event(f"A_sd_{tt}", s_ac)

            def v_st2(v, tt=tt, xtile=xtile):
                P.wait(v, f"A_sd_{tt}")
                if tt >= 1:
                    P.wait(v, f"A_trh_{tt - 1}")
                v.reciprocal(out=st_rs[:], in_=st_s1[:])
                v.reciprocal(out=st_c2[:], in_=st_s2[:])
                v.drain()
                v.tensor_scalar(out=afin[:, tt:tt + 1], in0=st_rs[:], scalar1=1.0,
                                scalar2=None, op0=ALU.add)
                v.tensor_tensor(out=bfin[:, tt:tt + 1], in0=st_m[:], in1=st_rs[:],
                                op=ALU.mult)
                v.drain()
                v.tensor_scalar(out=bfin[:, tt:tt + 1], in0=bfin[:, tt:tt + 1],
                                scalar1=-1.0, scalar2=None, op0=ALU.mult)
                v.tensor_scalar(out=hb[:], in0=xtile, scalar1=st_m[:],
                                scalar2=st_c2[:], op0=ALU.subtract, op1=ALU.mult)
                v.tensor_scalar(out=xnb[:], in0=xtile, scalar1=st_m[:],
                                scalar2=st_rs[:], op0=ALU.subtract, op1=ALU.mult)
                v.drain()
                v.sem_inc(s_dv.h, 1)
            P.emit("vector", v_st2)
            P.event(f"A_hx_{tt}", s_dv)
            P.alias(f"A_xtfree_{tt}", f"A_hx_{tt}")

            quant_emit(f"A_xq_{tt}", xnb[:], -9, 2.0 ** 12,
                       [f"A_hx_{tt}", "dat_bc"], None,
                       reader_ev=f"A_trxmm_{tt}")

            def t_trh(t, tt=tt, st=st, lc=lc, pbt_pred=pbt_pred):
                P.wait(t, f"A_hx_{tt}")
                if tt >= 1:
                    P.wait(t, f"A_hTev_{tt - 1}")
                if pbt_pred is not None and f"D_phiev_{pbt_pred}_3" in P.ev:
                    P.wait(t, f"D_phiev_{pbt_pred}_3")
                if tt == 4 and "wqtc_7" in P.ev:
                    P.wait(t, "wqtc_7")
                for j in range(8):
                    t.transpose(out=pbt[:, j * 128:(j + 1) * 128],
                                in_=hb[:, j * 128:(j + 1) * 128], identity=eye[:])
                t.drain()
                t.sem_inc(s_pe.h, 1)
            P.emit("tensor", t_trh)
            P.event(f"A_trh_{tt}", s_pe)

            def v_trevh(v, tt=tt, lc=lc, st=st):
                dst_h = hT[:].rearrange("p (d c) -> p d c", c=512)
                if st >= 1 and lc == 0:
                    P.wait(v, f"B_vmm_{st - 1}_1_7")
                    if tt == 4 and "wqt_7" in P.ev:
                        P.wait(v, "wqt_7")
                P.wait(v, f"A_trh_{tt}")
                v.tensor_copy(
                    out=dst_h[:, :, lc * 128:(lc + 1) * 128],
                    in_=pbt[:].rearrange("p (j c) -> p j c", c=128))
                v.drain()
                v.sem_inc(s_dv.h, 1)
            P.emit("vector", v_trevh)
            P.event(f"A_hTev_{tt}", s_dv)

        def emit_trx(tt):
            par = xq_par[f"A_xq_{tt}"]
            lc = tt % 4
            st = tt // 4

            def t_trx(t, tt=tt, par=par):
                P.wait(t, f"A_xq_{tt}")
                if tt % 4 == 0:
                    P.wait(t, f"A_hTev_{tt + 3}")
                else:
                    P.wait(t, f"A_trx_{tt - 1}")
                for j in range(8):
                    t.transpose(out=pbt[:, j * 128:(j + 1) * 128],
                                in_=xqb[:, par * D + j * 128:
                                        par * D + (j + 1) * 128],
                                identity=eye[:])
                t.drain()
                t.sem_inc(s_pe.h, 1)
            P.emit("tensor", t_trx)
            P.event(f"A_trxmm_{tt}", s_pe)

            def a_trevx(a, tt=tt, lc=lc, st=st):
                P.wait(a, f"A_trxmm_{tt}")
                if st >= 1 and lc == 0:
                    P.wait(a, f"E_ffmm_{st - 1}_1_7")
                dst_x = xqT[:].rearrange("p (d c) -> p d c", c=512)
                a.activation(out=dst_x[:, :, lc * 128:(lc + 1) * 128],
                             in_=pbt[:].rearrange("p (j c) -> p j c", c=128),
                             func=AF.Copy)
                a.drain()
                a.sem_inc(s_ac.h, 1)
            P.emit("scalar", a_trevx)
            P.event(f"A_trx_{tt}", s_ac)

        def phase_B(st):
            for tagp, wsrc, dstbuf in (("q", wqs_h, qT), ("k", wks_h, kT)):
                for mo in range(8):
                    slot = mo % 2
                    wv = wbf[:, slot * D:(slot + 1) * D]

                    dep = slot_user.get(("wbf", slot), "prologue")

                    def s_wp(sy, mo=mo, wsrc=wsrc, wv=wv, dep=dep, st=st,
                             tagp=tagp):
                        P.wait(sy, dep)
                        if st == 0:
                            P.wait(sy, "wscr_7" if tagp == "q" else "wscr_15")
                        sy.dma_start(
                            out=wv.rearrange("p (a c) -> p a c", a=8),
                            in_=wsrc[:, mo * 128:(mo + 1) * 128]
                            .rearrange("(a p) c -> p a c", p=128)
                        ).then_inc(s_wb[mo % 2].h, 16)
                    P.emit("sync", s_wp)
                    P.event(f"B_wl_{tagp}_{st}_{mo}", s_wb[mo % 2], 16)
                    slot_user[("wbf", slot)] = f"B_{tagp}mm_{st}_{mo}"

                    def t_p(t, st=st, mo=mo, tagp=tagp, wv=wv):
                        bank = mo % 2
                        P.wait(t, f"A_hTev_{st * 4 + 3}")
                        P.wait(t, f"B_wl_{tagp}_{st}_{mo}")
                        if st >= 1 and tagp == "q" and mo < 2:
                            P.wait(t, f"E_res_{st * 4 - 2 + mo}")
                        if mo >= 2:
                            P.wait(t, f"B_{tagp}ev_{st}_{mo - 2}")
                        elif tagp == "k":
                            P.wait(t, f"B_qev_{st}_{6 + mo}")
                        for kc in range(8):
                            t.matmul(pf[bank][:], wv[:, kc * 128:(kc + 1) * 128],
                                     hT[:, kc * 512:(kc + 1) * 512],
                                     start=(kc == 0), stop=(kc == 7))
                        t.drain()
                        t.sem_inc(s_pe.h, 1)
                    P.emit("tensor", t_p)
                    P.event(f"B_{tagp}mm_{st}_{mo}", s_pe)

                    def v_pe(v, st=st, mo=mo, tagp=tagp, dstbuf=dstbuf):
                        bank = mo % 2
                        P.wait(v, f"B_{tagp}mm_{st}_{mo}")
                        if st >= 1 and mo == 0:
                            P.wait(v, f"C_av_{st - 1}_3")
                            P.wait(v, f"D_kv_{st * 4 - 1}")
                        v.tensor_copy(out=dstbuf[:, mo * 512:(mo + 1) * 512],
                                      in_=pf[bank][:])
                        v.drain()
                        v.sem_inc(s_dv.h, 1)
                    P.emit("vector", v_pe)
                    P.event(f"B_{tagp}ev_{st}_{mo}", s_dv)
                    if mo >= 2:
                        drip(1)
                    if mo in (3, 7):
                        emit_trx(st * 4 + (0 if tagp == "q" else 2)
                                 + (0 if mo == 3 else 1))

            for half in range(2):
                for kc in range(8):
                    slot = kc % 2
                    wv2 = wbf2[:, slot * D:slot * D + 512]

                    dep = slot_user.get(("wbf2", slot), "prologue")

                    def s_wv(sy, half=half, kc=kc, wv2=wv2, dep=dep, st=st):
                        P.wait(sy, dep)
                        if st == 0:
                            P.wait(sy, "wscr_23")
                        sy.dma_start(out=wv2,
                                     in_=wvs_h[kc * 128:(kc + 1) * 128,
                                               half * 512:(half + 1) * 512]
                                     ).then_inc(s_v2[kc % 2].h, 16)
                    P.emit("sync", s_wv)
                    P.event(f"B_wlv_{st}_{half}_{kc}", s_v2[kc % 2], 16)
                    slot_user[("wbf2", slot)] = f"B_vmm_{st}_{half}_{kc}"

                    def t_v(t, st=st, half=half, kc=kc, wv2=wv2):
                        P.wait(t, f"B_wlv_{st}_{half}_{kc}")
                        if kc == 0 and half == 0:
                            P.wait(t, f"A_hTev_{st * 4 + 3}")
                            if st >= 1:
                                P.wait(t, f"E_res_{st * 4 - 1}")
                        if kc == 0:
                            pr = f"B_vev_{st}_{half - 1}_3" if half == 1 else None
                            if pr and pr in P.ev:
                                P.wait(t, pr)
                            if half == 0:
                                P.wait(t, f"B_kev_{st}_6")
                                P.wait(t, f"B_kev_{st}_7")
                        for lc in range(4):
                            t.matmul(pf[lc][:],
                                     hT[:, kc * 512 + lc * 128:kc * 512 + (lc + 1) * 128],
                                     wv2, start=(kc == 0), stop=(kc == 7))
                        t.drain()
                        t.sem_inc(s_pe.h, 1)
                    P.emit("tensor", t_v)
                    P.event(f"B_vmm_{st}_{half}_{kc}", s_pe)

                for lc in range(4):
                    def v_vev(v, st=st, half=half, lc=lc):
                        P.wait(v, f"B_vmm_{st}_{half}_7")
                        if st >= 1 and half == 0:
                            P.wait(v, f"D_kv_{st * 4 - 1}")
                            P.wait(v, f"C_av_{st - 1}_3")
                        dst = v65[:, lc * VW:(lc + 1) * VW] \
                            .rearrange("p (h c) -> p h c", c=65)[:, half * 8:half * 8 + 8, 0:64]
                        v.tensor_copy(out=dst,
                                      in_=pf[lc][:].rearrange("p (h c) -> p h c", c=64))
                        if half == 1:
                            ones = v65[:, lc * VW:(lc + 1) * VW] \
                                .rearrange("p (h c) -> p h c", c=65)[:, :, 64:65]
                            v.memset(ones, 1.0)
                        v.drain()
                        v.sem_inc(s_dv.h, 1)
                    P.emit("vector", v_vev)
                    P.event(f"B_vev_{st}_{half}_{lc}", s_dv)
                    drip(1)

            for si, (tsrc, fdst, colofs) in enumerate(((qT, qftm, 0), (kT, kftm, 32))):
                for lc in range(4):
                    bank = lc % 2

                    def t_qf(t, st=st, si=si, lc=lc, tsrc=tsrc, colofs=colofs):
                        bank = lc % 2
                        P.wait(t, f"B_{'q' if si == 0 else 'k'}ev_{st}_7")
                        P.wait(t, "pro0")
                        if si == 0 and lc == 0:
                            P.wait(t, f"B_vev_{st}_1_3")
                        pr = f"B_fev_{st}_{si}_{lc - 2}"
                        if pr in P.ev:
                            P.wait(t, pr)
                        elif si == 1:
                            pr2 = f"B_fev_{st}_0_{2 + lc}"
                            if pr2 in P.ev:
                                P.wait(t, pr2)
                        for pair in range(8):
                            t.matmul(pf[bank][:, pair * 32:(pair + 1) * 32],
                                     tsrc[:, pair * 512 + lc * 128:pair * 512 + (lc + 1) * 128],
                                     wqfbd[:, colofs:colofs + 32],
                                     start=True, stop=True)
                        t.drain()
                        t.sem_inc(s_pe.h, 1)
                    P.emit("tensor", t_qf)
                    P.event(f"B_fmm_{st}_{si}_{lc}", s_pe)

                    def v_fev(v, st=st, si=si, lc=lc, fdst=fdst):
                        bank = lc % 2
                        P.wait(v, f"B_fmm_{st}_{si}_{lc}")
                        if st >= 1 and si == 0 and lc == 0:
                            P.wait(v, f"D_kv_{st * 4 - 1}")
                        src = pf[bank][:, 0:256].rearrange("p (a e i) -> p a e i",
                                                           e=2, i=16)
                        dst = fdst[:, lc * 512:(lc + 1) * 512] \
                            .rearrange("p (a e s i) -> p a e s i", e=2, s=2, i=16)
                        v.tensor_copy(out=dst[:, :, :, 0, :], in_=src)
                        v.drain()
                        sqv = fdst[:, lc * 512:(lc + 1) * 512] \
                            .rearrange("p (hh s i) -> p hh s i", s=2, i=16)
                        v.scalar_tensor_tensor(out=sqv[:, :, 1, :],
                                               in0=sqv[:, :, 0, :],
                                               scalar=(0.25 if si == 1 else 1.0),
                                               in1=sqv[:, :, 0, :],
                                               op0=ALU.mult, op1=ALU.mult)
                        v.drain()
                        v.sem_inc(s_dv.h, 1)
                    P.emit("vector", v_fev)
                    P.event(f"B_fev_{st}_{si}_{lc}", s_dv)
                    drip(1)

            for si, tsrc in ((0, qT), (1, kT)):
                for pair in range(8):
                    gi = si * 8 + pair
                    sqb = sqs if gi % 2 == 0 else sqs2
                    mxb = pf[3] if gi % 2 == 0 else pf[2]
                    stc = gi % 2

                    def pr2(gi=gi, st=st):
                        g2 = gi - 2
                        if g2 < 0:
                            return None
                        return f"B_mx_{st}_{g2 // 8}_{g2 % 8}"

                    def v_sq(v, st=st, si=si, pair=pair, tsrc=tsrc, sqb=sqb,
                             pr2=pr2):
                        P.wait(v, f"B_{'q' if si == 0 else 'k'}ev_{st}_7")
                        pr = pr2()
                        if pr and pr in P.ev:
                            P.wait(v, pr)
                        v.tensor_mul(out=sqb[:],
                                     in0=tsrc[:, pair * 512:(pair + 1) * 512],
                                     in1=tsrc[:, pair * 512:(pair + 1) * 512])
                        v.drain()
                        v.sem_inc(s_dv.h, 1)
                    P.emit("vector", v_sq)
                    P.event(f"B_sqs_{st}_{si}_{pair}", s_dv)

                    def t_sq1(t, st=st, si=si, pair=pair, gi=gi, sqb=sqb,
                              mxb=mxb, pr2=pr2):
                        P.wait(t, f"B_sqs_{st}_{si}_{pair}")
                        if gi == 0:
                            P.wait(t, f"B_vev_{st}_1_3")
                        if gi == 1:
                            P.wait(t, f"B_vev_{st}_1_2")
                        pr = pr2()
                        if pr and pr in P.ev:
                            P.wait(t, pr)
                        t.matmul(mxb[0:2, 0:512], g2t[:], sqb[:],
                                 start=True, stop=True)
                        t.drain()
                        t.sem_inc(s_pe.h, 1)
                    P.emit("tensor", t_sq1)
                    P.event(f"B_sqmm_{st}_{si}_{pair}", s_pe)

                    def v_mx(v, st=st, si=si, pair=pair, mxb=mxb, stc=stc):
                        P.wait(v, f"B_sqmm_{st}_{si}_{pair}")
                        v.tensor_reduce(out=st_t[0:2, stc:stc + 1],
                                        in_=mxb[0:2, 0:512],
                                        axis=AX.X, op=ALU.max)
                        v.drain()
                        col = si * 9 + pair
                        v.tensor_tensor(out=mxs[0:2, col:col + 1],
                                        in0=mxs[0:2, col:col + 1],
                                        in1=st_t[0:2, stc:stc + 1], op=ALU.max)
                        v.drain()
                        v.sem_inc(s_dv.h, 1)
                    P.emit("vector", v_mx)
                    P.event(f"B_mx_{st}_{si}_{pair}", s_dv)
                    drip(1)

            def v_cb1(v, st=st):
                P.wait(v, f"B_mx_{st}_1_7")
                v.tensor_reduce(out=mxs[0:2, 8:9], in_=mxs[0:2, 0:8], axis=AX.X,
                                op=ALU.max)
                v.tensor_reduce(out=mxs[0:2, 17:18], in_=mxs[0:2, 9:17], axis=AX.X,
                                op=ALU.max)
                v.drain()
                v.tensor_add(out=mxs[0:2, 18:19], in0=mxs[0:2, 8:9],
                             in1=mxs[0:2, 17:18])
                v.drain()
                v.sem_inc(s_dv.h, 1)
            P.emit("vector", v_cb1)
            P.event(f"B_cb1_{st}", s_dv)

            def t_cb(t, st=st):
                P.wait(t, f"B_cb1_{st}")
                P.wait(t, f"B_vev_{st}_1_2")
                t.transpose(out=pf[2][0:1, 0:2], in_=mxs[0:2, 18:19],
                            identity=eyef[0:2, 0:2])
                t.drain()
                t.sem_inc(s_pe.h, 1)
            P.emit("tensor", t_cb)
            P.event(f"B_cbt_{st}", s_pe)

            def v_cb2(v, st=st):
                P.wait(v, f"B_cbt_{st}")
                v.tensor_reduce(out=st_t[0:1, 0:1], in_=pf[2][0:1, 0:2], axis=AX.X,
                                op=ALU.max)
                v.drain()
                v.tensor_scalar(out=st_t[0:1, 0:1], in0=st_t[0:1, 0:1],
                                scalar1=-1.0 / 16.0, scalar2=10.0 * LN2,
                                op0=ALU.mult, op1=ALU.add)
                v.drain()
                v.sem_inc(s_dv.h, 1)
            P.emit("vector", v_cb2)
            P.event(f"B_cb2_{st}", s_dv)

            def a_cbd(a, st=st):
                P.wait(a, f"B_cb2_{st}")
                a.dma_start(out=sc2_h, in_=st_t[0:1, 0:1]).then_inc(s_cb.h, 16)
            P.emit("scalar", a_cbd)
            P.event(f"B_cbd_{st}", s_cb, 16)

            def s_cbb(sy, st=st):
                P.wait(sy, f"B_cbd_{st}")
                sy.dma_start(out=cbias[:], in_=sc2_h.broadcast_to([128, 1])
                             ).then_inc(s_cb.h, 16)
            P.emit("sync", s_cbb)
            P.event(f"B_cbias_{st}", s_cb, 16)

        def emit_psi(si, st, lc):
            c = st * 4 + lc
            fsrc, psi = ((qftm, psiq), (kftm, psik))[si]

            def v_psi(v, st=st, lc=lc, si=si, fsrc=fsrc, psi=psi, c=c):
                P.wait(v, f"B_fev_{st}_{si}_{lc}")
                dep = f"D_kv_{c - 1}" if si == 1 else f"D_phitr_{c - 1}"
                if c >= 1 and dep in P.ev:
                    P.wait(v, dep)
                base = lc * 512
                qv = fsrc[:, base:base + 512].rearrange(
                    "p (hh s i) -> p hh s i", s=2, i=16)
                pv = psi[:].rearrange("p (h c) -> p h c", c=128)
                off = 0
                for dlt in range(1, 16):
                    n = 16 - dlt
                    if si == 0:
                        v.tensor_tensor(out=pv[:, :, off:off + n],
                                        in0=qv[:, :, 0, 0:n],
                                        in1=qv[:, :, 0, dlt:dlt + n],
                                        op=ALU.mult)
                    else:
                        v.scalar_tensor_tensor(out=pv[:, :, off:off + n],
                                               in0=qv[:, :, 0, 0:n], scalar=0.5,
                                               in1=qv[:, :, 0, dlt:dlt + n],
                                               op0=ALU.mult, op1=ALU.mult)
                    off += n
                v.drain()
                v.sem_inc(s_dv.h, 1)
            P.emit("vector", v_psi)
            P.event(f"D_psi{si}_{c}", s_dv)

        def phase_C(st, lc):
            import os as _os
            _csub = int(_os.environ.get("KMK_CSUB", "99"))
            if lc >= int(_os.environ.get("KMK_CLC", "4")):
                return
            c = st * 4 + lc
            first = (c == 0)

            emit_psi(0, st, lc)

            import os as _os2
            for g in range(int(_os2.environ.get("KMK_CG", "4"))):
                def t_sc(t, st=st, lc=lc, g=g, first=first, c=c):
                    if g == 0:
                        import os as _os3
                        if _os3.environ.get("KMK_NOCB") != "1":
                            P.wait(t, f"B_cbias_{st}")
                        P.wait(t, f"B_vev_{st}_1_{lc}")
                        if lc == 0 and st >= 1:
                            P.wait(t, f"C_halo_{st - 1}")
                        if f"D_zl_{c - 1}" in P.ev:
                            P.wait(t, f"D_zl_{c - 1}")
                        if f"D_kva_{c - 1}" in P.ev:
                            P.wait(t, f"D_kva_{c - 1}")
                        if lc == 0:
                            P.wait(t, f"B_cb2_{st}")
                    pr = f"C_expB_{st}_{lc}_{g - 1}"
                    if pr in P.ev:
                        P.wait(t, pr)
                    elif not first and f"C_expA_{st}_{lc - 1 if lc else 0}_x" in P.ev:
                        pass
                    for hh in (0, 2, 1, 3):
                        h = g * 4 + hh
                        pair, e = h // 2, h % 2
                        t.matmul(pf[0][:, hh * 128:(hh + 1) * 128],
                                 kT[e * 64:(e + 1) * 64,
                                    pair * 512 + lc * 128:pair * 512 + (lc + 1) * 128],
                                 qT[e * 64:(e + 1) * 64,
                                    pair * 512 + lc * 128:pair * 512 + (lc + 1) * 128],
                                 start=True, stop=True)
                    t.drain()
                    t.sem_inc(s_pe.h, 1)
                    if not first:
                        pr = f"C_expA_{st}_{lc}_{g - 1}"
                        if pr in P.ev:
                            P.wait(t, pr)
                        for hh in (1, 3, 0, 2):
                            h = g * 4 + hh
                            pair, e = h // 2, h % 2
                            if lc == 0:
                                ksl = kThal[e * 64:(e + 1) * 64,
                                            pair * 128 + 64:pair * 128 + 128]
                            else:
                                ksl = kT[e * 64:(e + 1) * 64,
                                         pair * 512 + lc * 128 - 64:pair * 512 + lc * 128]
                            t.matmul(pf[1][64:128, hh * 128:(hh + 1) * 128], ksl,
                                     qT[e * 64:(e + 1) * 64,
                                        pair * 512 + lc * 128:pair * 512 + (lc + 1) * 128],
                                     start=True, stop=True)
                        t.drain()
                        t.sem_inc(s_pe.h, 1)
                P.emit("tensor", t_sc)
                P.event(f"C_scB_{st}_{lc}_{g}", s_pe)
                if not first:
                    P.event(f"C_scA_{st}_{lc}_{g}", s_pe)

                if _csub < 2:
                    continue
                def a_exp(a, st=st, lc=lc, g=g, first=first):
                    P.wait(a, f"C_scB_{st}_{lc}_{g}")
                    a.activation(out=attB[:, g * 512:(g + 1) * 512],
                                 in_=pf[0][:, 0:512], func=AF.Exp, scale=0.125,
                                 bias=cbias[:, 0:1])
                    a.drain()
                    a.sem_inc(s_ac.h, 1)
                    if not first:
                        P.wait(a, f"C_scA_{st}_{lc}_{g}")
                        a.activation(out=attA[64:128, g * 512:(g + 1) * 512],
                                     in_=pf[1][64:128, 0:512], func=AF.Exp, scale=0.125,
                                     bias=cbias[64:128, 0:1])
                        a.drain()
                        a.sem_inc(s_ac.h, 1)
                P.emit("scalar", a_exp)
                P.event(f"C_expB_{st}_{lc}_{g}", s_ac)
                if not first:
                    P.event(f"C_expA_{st}_{lc}_{g}", s_ac)

                if _csub < 3:
                    continue
                def v_msk(v, st=st, lc=lc, g=g, first=first):
                    P.wait(v, f"C_expB_{st}_{lc}_{g}")
                    bm = attB[:, g * 512:(g + 1) * 512].rearrange(
                        "p (hh c) -> p hh c", c=128)
                    v.tensor_tensor(out=bm, in0=bm,
                                    in1=maskB[:, None, :].broadcast_to([128, 4, 128]),
                                    op=ALU.mult)
                    v.drain()
                    v.sem_inc(s_dv.h, 1)
                    if not first:
                        P.wait(v, f"C_expA_{st}_{lc}_{g}")
                        am = attA[64:128, g * 512:(g + 1) * 512].rearrange(
                            "p (hh c) -> p hh c", c=128)
                        v.tensor_tensor(out=am, in0=am,
                                        in1=maskA[64:128, None, :]
                                        .broadcast_to([64, 4, 128]),
                                        op=ALU.mult)
                        v.drain()
                        v.sem_inc(s_dv.h, 1)
                P.emit("vector", v_msk)
                P.event(f"C_mskB_{st}_{lc}_{g}", s_dv)
                drip(1)
                if not first:
                    P.event(f"C_mskA_{st}_{lc}_{g}", s_dv)

            if _csub < 4:
                return
            def t_av(t, st=st, lc=lc, first=first):
                P.wait(t, f"C_expB_{st}_{lc}_3")
                for h in range(H):
                    g = h // 4
                    P.wait(t, f"C_mskB_{st}_{lc}_{g}")
                    if not first:
                        P.wait(t, f"C_mskA_{st}_{lc}_{g}")
                    if h < 7:
                        dst = pf[2][:, h * 65:(h + 1) * 65]
                    elif h < 14:
                        dst = pf[3][:, (h - 7) * 65:(h - 6) * 65]
                    else:
                        dst = pf[0][:, (h - 14) * 65:(h - 13) * 65]
                    vsrc = v65[:, lc * VW + h * 65:lc * VW + (h + 1) * 65]
                    if first:
                        t.matmul(dst, attB[:, h * 128:(h + 1) * 128], vsrc,
                                 start=True, stop=True)
                    else:
                        vprev = (v65hal[:, h * 65:(h + 1) * 65] if lc == 0 else
                                 v65[:, (lc - 1) * VW + h * 65:(lc - 1) * VW + (h + 1) * 65])
                        t.matmul(dst, attA[64:128, h * 128:(h + 1) * 128],
                                 vprev[64:128, :], start=True, stop=False)
                        t.matmul(dst, attB[:, h * 128:(h + 1) * 128], vsrc,
                                 start=False, stop=True)
                t.drain()
                t.sem_inc(s_pe.h, 1)
            P.emit("tensor", t_av)
            P.event(f"C_av_{st}_{lc}", s_pe)

            if _csub < 5:
                return
            def v_zw(v, st=st, lc=lc):
                P.wait(v, f"C_av_{st}_{lc}")
                if st >= 1:
                    P.wait(v, f"E_trE_{(st - 1) * 4 + lc}_1")
                for bi, (bank, hs, nh) in enumerate(((pf[2], 0, 7), (pf[3], 7, 7),
                                                     (pf[0], 14, 2))):
                    zsl = bank[:, 0:65 * nh].rearrange("p (h c) -> p h c", c=65)
                    v.tensor_copy(out=zs[:, hs:hs + nh, None], in_=zsl[:, :, 64:65])
                v.drain()
                v.reciprocal(out=rz[:], in_=zs[:])
                v.drain()
                for bi, (bank, hs, nh) in enumerate(((pf[2], 0, 7), (pf[3], 7, 7),
                                                     (pf[0], 14, 2))):
                    ysl = bank[:, 0:65 * nh].rearrange("p (h c) -> p h c", c=65)
                    dst = win_tm[:, lc * D + hs * 64:lc * D + (hs + nh) * 64] \
                        .rearrange("p (h c) -> p h c", c=64)
                    v.tensor_tensor(out=dst, in0=ysl[:, :, 0:64],
                                    in1=rz[:, hs:hs + nh, None]
                                    .broadcast_to([128, nh, 64]), op=ALU.mult)
                v.drain()
                v.sem_inc(s_dv.h, 1)
            P.emit("vector", v_zw)
            P.event(f"C_zw_{st}_{lc}", s_dv)
            drip(1)

        def emit_halo(st):
            def v_halo(v, st=st):
                P.wait(v, f"C_zw_{st}_3")
                if f"D_kv_{st * 4 + 3}" in P.ev:
                    P.wait(v, f"D_kv_{st * 4 + 3}")
                src = kT[:].rearrange("p (a c) -> p a c", c=512)[:, :, 384:512]
                v.tensor_copy(out=kThal[:].rearrange("p (a c) -> p a c", c=128),
                              in_=src)
                v.tensor_copy(out=v65hal[:], in_=v65[:, 3 * VW:4 * VW])
                v.drain()
                v.sem_inc(s_dv.h, 1)
            P.emit("vector", v_halo)
            P.event(f"C_halo_{st}", s_dv)

        def phase_D(st, lc):
            import os as _os
            _dsub = int(_os.environ.get("KMK_DSUB", "99"))
            c = st * 4 + lc
            first = (c == 0)
            last = (c == NT - 1)

            emit_psi(1, st, lc)

            if _dsub < 2:
                return
            def t_trD(t, st=st, lc=lc, c=c):
                P.wait(t, f"B_fev_{st}_0_{lc}")
                P.wait(t, f"B_fev_{st}_1_{lc}")
                if lc == 0 and f"A_trx_{st * 4 + 3}" in P.ev:
                    P.wait(t, f"A_trx_{st * 4 + 3}")
                if st == 0 and lc >= 1 and f"wqtc_{2 * lc - 1}" in P.ev:
                    P.wait(t, f"wqtc_{2 * lc - 1}")
                if st >= 1 and lc >= 1 \
                        and f"A_hTev_{(st + 1) * 4 + lc - 1}" in P.ev:
                    P.wait(t, f"A_hTev_{(st + 1) * 4 + lc - 1}")
                P.wait(t, f"C_zw_{st}_{lc}")
                if f"D_zl_{c - 1}" in P.ev:
                    P.wait(t, f"D_zl_{c - 1}")
                elif f"D_smm_{c - 1}_3" in P.ev:
                    P.wait(t, f"D_smm_{c - 1}_3")
                if f"C_expA_{st}_{lc}_3" in P.ev:
                    P.wait(t, f"C_expA_{st}_{lc}_3")
                for j in range(8):
                    t.transpose(out=pbt[0:64, j * 128:(j + 1) * 128],
                                in_=qftm[:, lc * 512 + j * 64:lc * 512 + (j + 1) * 64],
                                identity=eye[:])
                t.drain()
                t.sem_inc(s_pe.h, 1)
                P.wait(t, f"D_fmev_0_{c}")
                for j in range(8):
                    t.transpose(out=pbt[0:64, j * 128:(j + 1) * 128],
                                in_=kftm[:, lc * 512 + j * 64:lc * 512 + (j + 1) * 64],
                                identity=eye[:])
                t.drain()
                t.sem_inc(s_pe.h, 1)
            P.emit("tensor", t_trD)
            P.event(f"D_qftr_{c}", s_pe)
            P.event(f"D_kftr_{c}", s_pe)

            for si in range(2):
                def v_fmev(v, si=si, c=c):
                    P.wait(v, f"D_{'qf' if si == 0 else 'kf'}tr_{c}")
                    v.tensor_copy(out=(qfFM if si == 0 else kfFM)[:],
                                  in_=pbt[0:64, :])
                    v.drain()
                    v.sem_inc(s_dv.h, 1)
                P.emit("vector", v_fmev)
                P.event(f"D_fmev_{si}_{c}", s_dv)
                drip(1)

            if _dsub < 3:
                return
            def t_phitr(t, c=c):
                P.wait(t, f"D_psi0_{c}")
                P.wait(t, f"D_fmev_1_{c}")
                for r in range(4):
                    for j in range(4):
                        hh = r * 4 + j
                        t.transpose(out=pbt[:, j * 128:(j + 1) * 128],
                                    in_=psiq[:, hh * 128:(hh + 1) * 128],
                                    identity=eye[:])
                    t.drain()
                    t.sem_inc(s_pe.h, 1)
                    P.wait(t, f"D_phiev_{c}_{r}")
            P.emit("tensor", t_phitr)
            for r in range(4):
                P.event(f"D_phitr_{c}_{r}", s_pe)

                def v_phiev(v, c=c, r=r):
                    P.wait(v, f"D_phitr_{c}_{r}")
                    v.tensor_copy(out=phiq[:, r * 512:(r + 1) * 512],
                                  in_=pbt[:, 0:512])
                    v.drain()
                    v.sem_inc(s_dv.h, 1)
                P.emit("vector", v_phiev)
                P.event(f"D_phiev_{c}_{r}", s_dv)
                drip(1)
            P.alias(f"D_phitr_{c}", f"D_phiev_{c}_3")

            if _dsub < 4:
                return
            for g in range(4):
                sbank = pf[1] if g % 2 == 0 else pf[0]

                def t_s(t, c=c, g=g, st=st, lc=lc, sbank=sbank):
                    P.wait(t, f"D_fmev_0_{c}")
                    P.wait(t, f"D_fmev_1_{c}")
                    if g == 0:
                        if f"C_expA_{st}_{lc}_3" in P.ev:
                            P.wait(t, f"C_expA_{st}_{lc}_3")
                    if g == 1:
                        P.wait(t, f"C_expB_{st}_{lc}_3")
                    pr = f"D_sq_{c}_{g - 2}"
                    if pr in P.ev:
                        P.wait(t, pr)
                    for hh in ((0, 2, 1, 3) if g % 2 == 0 else (1, 3, 0, 2)):
                        h = g * 4 + hh
                        j, e = h // 2, h % 2
                        t.matmul(sbank[:, hh * 128:(hh + 1) * 128],
                                 kfFM[32 * e:32 * e + 16, j * 128:(j + 1) * 128],
                                 qfFM[32 * e:32 * e + 16, j * 128:(j + 1) * 128],
                                 start=True, stop=True)
                    t.drain()
                    t.sem_inc(s_pe.h, 1)
                P.emit("tensor", t_s)
                P.event(f"D_smm_{c}_{g}", s_pe)

                def a_sq(a, c=c, g=g, sbank=sbank):
                    P.wait(a, f"D_smm_{c}_{g}")
                    a.activation(out=atil[:, g * 512:(g + 1) * 512],
                                 in_=sbank[:, 0:512], func=AF.Square, scale=0.5,
                                 bias=bias_b[:, B_ONE:B_ONE + 1])
                    a.drain()
                    a.sem_inc(s_ac.h, 1)
                P.emit("scalar", a_sq)
                P.event(f"D_sq_{c}_{g}", s_ac)

                def v_tril(v, c=c, g=g):
                    P.wait(v, f"D_sq_{c}_{g}")
                    if c >= 1:
                        P.wait(v, f"D_ymm_{c - 1}")
                    am = atil[:, g * 512:(g + 1) * 512].rearrange(
                        "p (hh x) -> p hh x", x=128)
                    v.tensor_tensor(out=am, in0=am,
                                    in1=trilq[:, None, :].broadcast_to([128, 4, 128]),
                                    op=ALU.mult)
                    v.drain()
                    v.sem_inc(s_dv.h, 1)
                P.emit("vector", v_tril)
                P.event(f"D_tril_{c}_{g}", s_dv)
                drip(1)


            if _dsub < 5:
                return
            def t_y(t, c=c, st=st, lc=lc, first=first):
                if f"D_phiev_{c}_3" in P.ev:
                    P.wait(t, f"D_phiev_{c}_3")
                for g in range(4):
                    P.wait(t, f"D_tril_{c}_{g}")
                if (not first) and f"D_kvev_{c - 1}" in P.ev:
                    P.wait(t, f"D_kvev_{c - 1}")
                P.wait(t, f"C_zw_{st}_{lc}")
                for h in range(H):
                    if h < 7:
                        dst = pf[2][:, h * 65:(h + 1) * 65]
                    elif h < 14:
                        dst = pf[3][:, (h - 7) * 65:(h - 6) * 65]
                    else:
                        dst = pf[1][:, (h - 14) * 65:(h - 13) * 65]
                    vsrc = v65[:, lc * VW + h * 65:lc * VW + (h + 1) * 65]
                    t.matmul(dst, atil[:, h * 128:(h + 1) * 128], vsrc,
                             start=True, stop=first)
                    if not first:
                        t.matmul(dst, phiq[:, h * 128:(h + 1) * 128],
                                 kv_ab[:, h * 65:(h + 1) * 65], start=False,
                                 stop=False)
                        t.matmul(dst, qfFM[32 * (h % 2):32 * (h % 2) + 32,
                                           (h // 2) * 128:(h // 2 + 1) * 128],
                                 kv_bb[32 * (h % 2):32 * (h % 2) + 32,
                                       (h // 2) * 65:(h // 2 + 1) * 65],
                                 start=False, stop=True)
                t.drain()
                t.sem_inc(s_pe.h, 1)
            P.emit("tensor", t_y)
            P.event(f"D_ymm_{c}", s_pe)

            if _dsub < 6:
                return
            def v_zl(v, c=c, lc=lc, st=st):
                P.wait(v, f"D_ymm_{c}")
                if st >= 1:
                    P.wait(v, f"E_trE_{(st - 1) * 4 + lc}_1")
                for bi, (bank, hs, nh) in enumerate(((pf[2], 0, 7), (pf[3], 7, 7),
                                                     (pf[1], 14, 2))):
                    zsl = bank[:, 0:65 * nh].rearrange("p (h x) -> p h x", x=65)
                    v.tensor_copy(out=zs[:, hs:hs + nh, None], in_=zsl[:, :, 64:65])
                v.drain()
                v.tensor_scalar(out=zs[:], in0=zs[:], scalar1=1e-6, scalar2=None,
                                op0=ALU.add)
                v.drain()
                v.reciprocal(out=rz[:], in_=zs[:])
                v.drain()
                for bi, (bank, hs, nh) in enumerate(((pf[2], 0, 7), (pf[3], 7, 7),
                                                     (pf[1], 14, 2))):
                    ysl = bank[:, 0:65 * nh].rearrange("p (h x) -> p h x", x=65)
                    dst = lin_tm[:, lc * D + hs * 64:lc * D + (hs + nh) * 64] \
                        .rearrange("p (h x) -> p h x", x=64)
                    v.tensor_tensor(out=dst, in0=ysl[:, :, 0:64],
                                    in1=rz[:, hs:hs + nh, None]
                                    .broadcast_to([128, nh, 64]), op=ALU.mult)
                v.drain()
                v.sem_inc(s_dv.h, 1)
            P.emit("vector", v_zl)
            P.event(f"D_zl_{c}", s_dv)
            drip(1)

            if _dsub < 7:
                return
            def t_kv(t, c=c, lc=lc, first=first):
                P.wait(t, f"D_psi1_{c}")
                P.wait(t, f"D_sq_{c}_3")
                if (not first) and f"D_kvev_{c - 1}" in P.ev:
                    P.wait(t, f"D_kvev_{c - 1}")
                def vsl(h):
                    return v65[:, lc * VW + h * 65:lc * VW + (h + 1) * 65]

                def kfl(h):
                    return kftm[:, lc * 512 + h * 32:lc * 512 + h * 32 + 32]

                t.matmul(pf[0][:, 0:65], psik[:, 14 * 128:15 * 128], vsl(14),
                         start=True, stop=False)
                t.matmul(pf[0][0:32, 130:195], kfl(14), vsl(14),
                         start=False, stop=False, skip_group_check=True)
                t.matmul(pf[0][32:64, 130:195], kfl(15), vsl(15),
                         start=False, stop=False, skip_group_check=True)
                t.matmul(pf[0][:, 65:130], psik[:, 15 * 128:16 * 128], vsl(15),
                         start=False, stop=True)
                t.drain()
                t.sem_inc(s_pe.h, 1)
                for h in range(14):
                    if h < 7:
                        dst = kva_p[:, h * 65:(h + 1) * 65]
                        st_, sp_ = (h == 0), (h == 6)
                    else:
                        dst = kva_p2[:, (h - 7) * 65:(h - 6) * 65]
                        st_, sp_ = (h == 7), (h == 13)
                    t.matmul(dst, psik[:, h * 128:(h + 1) * 128], vsl(h),
                             start=st_, stop=sp_)
                    dstb = kvmix[32 * (h % 2):32 * (h % 2) + 32,
                                 (h // 2) * 65:(h // 2 + 1) * 65]
                    t.matmul(dstb, kfl(h), vsl(h),
                             start=(h <= 1), stop=(h >= 12),
                             skip_group_check=True)
                t.drain()
                t.sem_inc(s_pe.h, 1)
            P.emit("tensor", t_kv)
            P.event(f"D_kvp0_{c}", s_pe)
            P.event(f"D_kv_{c}", s_pe)

            def v_kvev_a(v, c=c, first=first):
                P.wait(v, f"D_kvp0_{c}")
                if first:
                    v.tensor_copy(out=kva_sb[:, 910:1040], in_=pf[0][:, 0:130])
                    v.tensor_copy(out=kvb_sb[:, 455:520], in_=pf[0][0:64, 130:195])
                else:
                    v.tensor_add(out=kva_sb[:, 910:1040], in0=kva_sb[:, 910:1040],
                                 in1=pf[0][:, 0:130])
                    v.tensor_add(out=kvb_sb[:, 455:520], in0=kvb_sb[:, 455:520],
                                 in1=pf[0][0:64, 130:195])
                v.drain()
                v.tensor_copy(out=kv_ab[:, 910:1040], in_=kva_sb[:, 910:1040])
                v.tensor_copy(out=kv_bb[:, 455:520], in_=kvb_sb[:, 455:520])
                v.drain()
                v.sem_inc(s_dv.h, 1)
            P.emit("vector", v_kvev_a)
            P.event(f"D_kva_{c}", s_dv)
            drip(1)

            def v_kvev(v, c=c, first=first):
                P.wait(v, f"D_kv_{c}")
                if first:
                    v.tensor_copy(out=kva_sb[:, 0:455], in_=kva_p[:])
                    v.tensor_copy(out=kva_sb[:, 455:910], in_=kva_p2[:])
                    v.tensor_copy(out=kvb_sb[:, 0:455], in_=kvmix[0:64, 0:455])
                else:
                    v.tensor_add(out=kva_sb[:, 0:455], in0=kva_sb[:, 0:455],
                                 in1=kva_p[:])
                    v.tensor_add(out=kva_sb[:, 455:910], in0=kva_sb[:, 455:910],
                                 in1=kva_p2[:])
                    v.tensor_add(out=kvb_sb[:, 0:455], in0=kvb_sb[:, 0:455],
                                 in1=kvmix[0:64, 0:455])
                v.drain()
                v.tensor_copy(out=kv_ab[:, 0:910], in_=kva_sb[:, 0:910])
                v.tensor_copy(out=kv_bb[:, 0:455], in_=kvb_sb[:, 0:455])
                v.drain()
                v.sem_inc(s_dv.h, 1)
            P.emit("vector", v_kvev)
            P.event(f"D_kvev_{c}", s_dv)
            drip(1)

        def phase_E(st):
            for pr_i in range(2):
                tiles = (pr_i * 2, pr_i * 2 + 1)
                for ti, tloc in enumerate(tiles):
                    tt = st * 4 + tloc
                    slot = ti

                    def t_trE(t, st=st, tloc=tloc, slot=slot, tt=tt, pr_i=pr_i):
                        P.wait(t, f"D_zl_{st * 4 + tloc}")
                        P.wait(t, f"C_zw_{st}_{tloc}")
                        if st == 0 and pr_i == 0 and slot == 0 \
                                and "wqtc_7" in P.ev:
                            P.wait(t, "wqtc_7")
                        if pr_i == 0 and slot == 0 \
                                and f"A_hTev_{st * 4 + 7}" in P.ev:
                            P.wait(t, f"A_hTev_{st * 4 + 7}")
                        for r in range(2):
                            for j in range(8):
                                fj = r * 8 + j
                                srcb = (lin_tm if fj < 8 else win_tm)
                                jj = fj % 8
                                t.transpose(out=pbt[:, j * 128:(j + 1) * 128],
                                            in_=srcb[:, tloc * D + jj * 128:
                                                     tloc * D + (jj + 1) * 128],
                                            identity=eye[:])
                            t.drain()
                            t.sem_inc(s_pe.h, 1)
                            P.wait(t, f"E_fmevr_{tt}_{r}")
                    P.emit("tensor", t_trE)
                    for r in range(2):
                        P.event(f"E_trE_{tt}_{r}", s_pe)

                        def v_fmev(v, tt=tt, r=r, slot=slot):
                            P.wait(v, f"E_trE_{tt}_{r}")
                            v.tensor_copy(out=fmst[:, slot * 2048 + r * 1024:
                                                   slot * 2048 + (r + 1) * 1024],
                                          in_=pbt[:])
                            v.drain()
                            v.sem_inc(s_dv.h, 1)
                        P.emit("vector", v_fmev)
                        P.event(f"E_fmevr_{tt}_{r}", s_dv)
                    P.alias(f"E_fmev_{st}_{pr_i}_{ti}", f"E_fmevr_{tt}_1")
                    P.alias(f"E_trE_{tt}_3", f"E_trE_{tt}_1")

                obanks = ((pf[0], pf[1]), (pf[2], pf[3]))

                def t_gemm(t, st=st, pr_i=pr_i, tiles=tiles, obanks=obanks):
                    for ti in range(2):
                        P.wait(t, f"E_fmev_{st}_{pr_i}_{ti}")
                    P.wait(t, f"A_trx_{st * 4 + 3}")
                    if pr_i == 0:
                        P.wait(t, f"D_zl_{st * 4 + 3}")
                        P.wait(t, f"C_zw_{st}_3")
                    else:
                        P.wait(t, f"E_res_{st * 4 + 1}")
                    for fc in range(16):
                        P.wait(t, f"E_wol_{st}_{pr_i}_{fc}")
                        for ti in range(2):
                            lhs = fmst[:, ti * 2048 + fc * 128:ti * 2048 + (fc + 1) * 128]
                            for half in range(2):
                                t.matmul(obanks[ti][half][:, 0:512], lhs,
                                         wbf[:, (fc % 2) * D + half * 512:
                                             (fc % 2) * D + (half + 1) * 512],
                                         start=(fc == 0), stop=False)
                        t.drain()
                        t.sem_inc(s_pe.h, 1)
                    for dc in range(8):
                        P.wait(t, f"E_wql_{st}_{pr_i}_{dc}")
                        for ti, tloc in enumerate(tiles):
                            lhs = xqT[:, dc * 512 + tloc * 128:dc * 512 + (tloc + 1) * 128]
                            for half in range(2):
                                t.matmul(obanks[ti][half][:, 0:512], lhs,
                                         wbf2[:, (dc % 2) * D + half * 512:
                                              (dc % 2) * D + (half + 1) * 512],
                                         start=False, stop=False)
                        t.drain()
                        t.sem_inc(s_pe.h, 1)
                    P.wait(t, "dact0")
                    for ti in range(2):
                        for half in range(2):
                            t.matmul(obanks[ti][half][:, 0:512], ones1r[:],
                                     fbt[:, half * 512:(half + 1) * 512],
                                     start=False, stop=True)
                    t.drain()
                    t.sem_inc(s_pe.h, 1)
                P.emit("tensor", t_gemm)
                for fc in range(16):
                    P.event(f"E_womm_{st}_{pr_i}_{fc}", s_pe)
                for dc in range(8):
                    P.event(f"E_ffmm_{st}_{pr_i}_{dc}", s_pe)
                P.event(f"E_bias_{st}_{pr_i}", s_pe)

                for fc in range(16):
                    dep = slot_user.get(("wbf", fc % 2), "prologue")

                    def s_wo(sy, fc=fc, dep=dep, st=st):
                        P.wait(sy, dep)
                        if st == 0:
                            P.wait(sy, "stg_done")
                        sy.dma_start(out=wbf[:, (fc % 2) * D:(fc % 2) * D + D],
                                     in_=wos_h[fc * 128:(fc + 1) * 128, :]
                                     ).then_inc(s_wb[fc % 2].h, 16)
                    P.emit("sync", s_wo)
                    P.event(f"E_wol_{st}_{pr_i}_{fc}", s_wb[fc % 2], 16)
                    slot_user[("wbf", fc % 2)] = f"E_womm_{st}_{pr_i}_{fc}"
                for dc in range(8):
                    dep = slot_user.get(("wbf2", dc % 2), "prologue")

                    def s_wq(sy, dc=dc, dep=dep, st=st):
                        P.wait(sy, dep)
                        if st == 0:
                            P.wait(sy, "prologue")
                        sy.dma_start(out=wbf2[:, (dc % 2) * D:(dc % 2) * D + D],
                                     in_=wqt_h[dc * 128:(dc + 1) * 128, :]
                                     ).then_inc(s_v2[dc % 2].h, 16)
                    P.emit("scalar", s_wq)
                    P.event(f"E_wql_{st}_{pr_i}_{dc}", s_v2[dc % 2], 16)
                    slot_user[("wbf2", dc % 2)] = f"E_ffmm_{st}_{pr_i}_{dc}"

                for ti, tloc in enumerate(tiles):
                    tt = st * 4 + tloc

                    def s_xr_f(sy, tt=tt, st=st, pr_i=pr_i, ti=ti):
                        pr = f"E_res_{tt - 1}"
                        if pr in P.ev:
                            P.wait(sy, pr)
                        sy.dma_start(out=xres[:], in_=x_h[tt * 128:(tt + 1) * 128, :]
                                     ).then_inc(s_xr.h, 16)
                    P.emit("scalar", s_xr_f)
                    P.event(f"E_xr_{tt}", s_xr, 16)

                    def v_res(v, tt=tt, st=st, pr_i=pr_i, ti=ti, obanks=obanks):
                        P.wait(v, f"E_xr_{tt}")
                        if tt == 0:
                            P.wait(v, "oout_free")
                        if f"E_store_{tt - 1}" in P.ev:
                            P.wait(v, f"E_store_{tt - 1}")
                        v.tensor_scalar(out=oout[:], in0=xres[:],
                                        scalar1=afin[:, tt:tt + 1],
                                        scalar2=bfin[:, tt:tt + 1],
                                        op0=ALU.mult, op1=ALU.add)
                        v.drain()
                        P.wait(v, f"E_bias_{st}_{pr_i}")
                        for half in range(2):
                            v.tensor_tensor(out=oout[:, half * 512:(half + 1) * 512],
                                            in0=oout[:, half * 512:(half + 1) * 512],
                                            in1=obanks[ti][half][:, 0:512],
                                            op=ALU.add)
                        v.drain()
                        v.sem_inc(s_dv.h, 1)
                    P.emit("vector", v_res)
                    P.event(f"E_res_{tt}", s_dv)

                    def a_store(a, tt=tt):
                        P.wait(a, f"E_res_{tt}")
                        a.dma_start(out=out_h[tt * 128:(tt + 1) * 128, :],
                                    in_=oout[:]).then_inc(s_so[tt % 2].h, 16)
                    P.emit("scalar", a_store)
                    P.event(f"E_store_{tt}", s_so[tt % 2], 16)
            P.alias(f"E_done_{st}", f"E_store_{st * 4 + 3}")

        import os as _os
        _lim = _os.environ.get("KMK_PHASES", "e")
        for st in range(NST):
            if _lim == "p":
                break
            if st == 0:
                emit_staging()
            if _lim == "a":
                continue
            phase_B(st)
            if _lim == "b":
                continue
            if st == 0 and _lim not in ("c",):
                emit_fwq_q(0)
                emit_fwq_q(1)
            for lc in range(4):
                if st == 0 and lc >= 1 and _lim not in ("c",):
                    emit_fwq_t(2 * (lc - 1), lc - 1)
                    emit_fwq_t(2 * lc - 1, lc - 1)
                phase_C(st, lc)
                if _lim != "c":
                    phase_D(st, lc)
                if st == 0 and lc <= 2 and _lim not in ("c",):
                    emit_fwq_q(2 * lc + 2)
                    emit_fwq_q(2 * lc + 3)
                if st >= 1 and st + 1 < NST and lc <= 2 and _lim == "e":
                    phase_A((st + 1) * 4 + lc, pbt_pred=st * 4 + lc)
            if st == 0 and _lim not in ("c", "d"):
                drip(99)
                emit_fwq_t(6, 3)
                emit_fwq_t(7, 3)
            if _lim in ("c", "d"):
                continue
            emit_halo(st)
            if st + 1 < NST:
                if st == 0:
                    phase_A((st + 1) * 4 + 0, pbt_pred=3)
                    for lc in range(1, 4):
                        phase_A((st + 1) * 4 + lc)
                else:
                    phase_A((st + 1) * 4 + 3, pbt_pred=st * 4 + 3)
            phase_E(st)

        def s_final(sy):
            for s in (s_so[0], s_so[1], s_vd, s_dv, s_ac, s_pe, s_gp, s_ld,
                      s_cb, s_xr, s_x2[0], s_x2[1], s_wb[0], s_wb[1],
                      s_v2[0], s_v2[1], s_sg[0], s_sg[1],
                      s_sv[0], s_sv[1], s_sv[2], s_sv[3]):
                if s.n:
                    sy.wait_ge(s.h, s.n)
        P.emit("sync", s_final)

        with nc.Block() as block:
            @block.sync
            def _(eng):
                for fn in P.ops["sync"]:
                    fn(eng)

            @block.gpsimd
            def _(eng):
                for fn in P.ops["gpsimd"]:
                    fn(eng)

            @block.vector
            def _(eng):
                for fn in P.ops["vector"]:
                    fn(eng)

            @block.scalar
            def _(eng):
                for fn in P.ops["scalar"]:
                    fn(eng)

            @block.tensor
            def _(eng):
                geng = PEGuard(eng, eye[:, 0:1])
                for fn in P.ops["tensor"]:
                    fn(geng)

    return nc


_CACHE = {}


def _get_program(T):
    if T not in _CACHE:
        _CACHE[T] = build_program(T)
    return _CACHE[T]


def kernel(L0_attn_x, L0_attn_norm_w, L0_attn_Wq, L0_attn_Wk, L0_attn_Wv,
           L0_attn_Wqf, L0_attn_Wkf, L0_attn_Wout, L0_ffn_weight, L0_ffn_bias,
           L0_ffn_delta_w, L0_ffn_delta_act):
    x = np.ascontiguousarray(np.asarray(L0_attn_x, dtype=np.float32))
    B, T, _ = x.shape
    nc = _get_program(T)
    shared = {
        "norm_w": np.ascontiguousarray(
            np.asarray(L0_attn_norm_w, dtype=np.float32).reshape(8, 128)),
        "Wq": np.ascontiguousarray(np.asarray(L0_attn_Wq, dtype=np.float32)),
        "Wk": np.ascontiguousarray(np.asarray(L0_attn_Wk, dtype=np.float32)),
        "Wv": np.ascontiguousarray(np.asarray(L0_attn_Wv, dtype=np.float32)),
        "Wqf": np.ascontiguousarray(np.asarray(L0_attn_Wqf, dtype=np.float32)),
        "Wkf": np.ascontiguousarray(np.asarray(L0_attn_Wkf, dtype=np.float32)),
        "Wout": np.ascontiguousarray(np.asarray(L0_attn_Wout, dtype=np.float32)),
        "fw": np.ascontiguousarray(np.asarray(L0_ffn_weight, dtype=np.float32)),
        "fb": np.ascontiguousarray(
            np.asarray(L0_ffn_bias, dtype=np.float32).reshape(1, D)),
        "dw": np.ascontiguousarray(np.asarray(L0_ffn_delta_w, dtype=np.float32)),
        "da": np.ascontiguousarray(
            np.asarray(L0_ffn_delta_act, dtype=np.float32).reshape(1, 1)),
    }
    in_maps = [dict(shared, x=np.ascontiguousarray(x[i])) for i in range(B)]
    res = run_bass_kernel_spmd(nc, in_maps, core_ids=list(range(B)))
    out = np.stack([np.asarray(r["out"], dtype=np.float32) for r in res.results], 0)
    return out
